# version 02
# baseline (speedup 1.0000x reference)
import numpy as np
import ml_dtypes
import concourse.bass as bass
import concourse.mybir as mybir
from concourse.bass_utils import run_bass_kernel_spmd

F32 = mybir.dt.float32
F32R = mybir.dt.float32r
BF16 = mybir.dt.bfloat16
AF = mybir.ActivationFunctionType
ALU = mybir.AluOpType
AX = mybir.AxisListType

NCORES = 8
NSEQ = 2
S = 2048
D = 1024
NCC = 8
H = 16
HP = 8
HD = 64
QB = 512
NQB = S // QB
NKB = S // 128
NE = 8
FF = 2048
NFC = FF // 128
NDC = D // 128
TT = 1024
LN_EPS = 1e-5

TOP = 212352
PER = 16640
OFF_VECS = PER
OFF_WRP = PER + 128
OFF_WRT = OFF_WRP + 256
OFF_RBIAS = OFF_WRT + 256
OFF_ONES32 = OFF_RBIAS + 32
OFF_ONES32C = OFF_ONES32 + 32
OFF_ONESR = OFF_ONES32C + 512
OFF_IDENT = OFF_ONESR + 512
OFF_SC1 = OFF_IDENT + 512
OFF_SC2 = OFF_SC1 + 576
PERSIST_END = OFF_SC2 + 320
QR = PERSIST_END
OFF_WBUF = QR
OFF_PB = QR + 16384
OFF_NB = OFF_PB + 8192
OFF_QSC = OFF_NB + 4096
QUANT_END = OFF_QSC + 64
OFF_GATEBC = QR
OFF_GROW = QR + 2048
OFF_GROWR = QR + 6144
U0 = ((QUANT_END + 63) // 64) * 64
assert U0 + 163840 <= TOP, (U0, TOP)

P0_XBUF = 0
P0_SQ = 32768
P0_UB = 36864
P0_U1 = 40960
P0_BCT = 43008
P0_XNST = 51200
P0_HST = 67584
P0_ROWS = 83968
P0_LOGIT = 100352
P0_SMALL = 116736

P1_QT = 0
P1_KT = 32768
P1_VP = 65536
P1_XO = 98816
P1_EB = 131584
P1_BCN = 135680
P1_RROW = 139776
P1_RROWR = 143872
P1A_WQF32 = 131584
P1A_WQBF = 139776
P1V_WVF32 = 131584
P1V_WVBF = 139776
P1C_WOF32 = 131584
P1C_WOBF = 139776
P1C_XSTR = 0
P1C_OUTST = 32768

P2_Q1 = 0
P2_Q2 = 32768
P2_H = 65536
P2_G1 = 98304
P2_ACC = 131072


class KB:
    ENGINES = ("sync", "scalar", "vector", "tensor", "gpsimd")

    def __init__(self, nc):
        self.nc = nc
        self.ops = []
        self.cnt = {}

    ENGSEM = {'vector': 'dve', 'scalar': 'act', 'gpsimd': 'pool'}

    def op(self, engine, fn, waits=(), inc=None, n=1):
        ws = []
        for s, v in waits:
            if v <= 0:
                continue
            assert v <= self.cnt.get(s, 0), (engine, s, v, self.cnt.get(s, 0))
            ws.append((s, v))
        es = self.ENGSEM.get(engine)
        if es is not None:
            prev = self.cnt.get(es, 0)
            if prev > 0 and not any(s == es and v >= prev for (s, v) in ws):
                ws.append((es, prev))
            if inc is None:
                inc, n = es, 1
        self.ops.append((engine, fn, tuple(ws), inc, n))
        if inc is not None:
            self.cnt[inc] = self.cnt.get(inc, 0) + n
            return self.cnt[inc]
        return None

    NQSEM = 6

    def dma(self, q, out, in_, waits=(), sem=None, transpose=False):
        if not hasattr(self, 'qrr'):
            self.qrr = {}
        i = self.qrr.get(q, 0)
        self.qrr[q] = (i + 1) % self.NQSEM
        snm = ('sq%d' if q == 'sync' else 'cq%d') % i
        w = list(waits)
        prev = self.cnt.get(snm, 0)
        if prev > 0:
            w.append((snm, prev))
        if transpose:
            fn = lambda e, o=out, i=in_: e.dma_start_transpose(o, i)
        else:
            fn = lambda e, o=out, i=in_: e.dma_start(out=o, in_=i)
        v = self.op(q, fn, w, inc=snm, n=16)
        return (snm, v)

    @staticmethod
    def maxtok(toks):
        best = {}
        for (nm, v) in toks:
            if v > best.get(nm, 0):
                best[nm] = v
        return list(best.items())

    def all_dma(self):
        toks = []
        for i in range(self.NQSEM):
            for pre in ('sq', 'cq'):
                nm = '%s%d' % (pre, i)
                if self.cnt.get(nm, 0) > 0:
                    toks.append((nm, self.cnt[nm]))
        return toks

    def c(self, sem):
        return self.cnt.get(sem, 0)

    def build(self, sems):
        used = {e for (e, *_r) in self.ops}
        with self.nc.Block() as blk:
            for en in self.ENGINES:
                if en not in used:
                    continue

                def mk(en):
                    def body(eng):
                        for (e, fn, waits, inc, n) in self.ops:
                            if e != en:
                                continue
                            for (s, v) in waits:
                                eng.wait_ge(sems[s], v)
                            ins = fn(eng)
                            if inc is not None:
                                ins.then_inc(sems[inc], n)
                    return body
                getattr(blk, en)(mk(en))


def build_nc(phases=("p0", "p1", "p2"), debug=False):
    nc = bass.Bass("TRN2", target_bir_lowering=False)
    resv = nc.alloc_sbuf_tensor("resv_manual", [128, 195840], mybir.dt.int8)
    rm = nc.lookup_mloc(resv)
    assert rm.addr <= PER and rm.addr + 195840 >= PER + 195712, rm.addr
    ikind = "ExternalOutput" if debug else "Internal"

    xT_d = nc.dram_tensor("xT", [NSEQ, D, S], F32, kind="ExternalInput").ap()
    wqkvT_d = nc.dram_tensor("wqkvT", [D, 3 * D], F32, kind="ExternalInput").ap()
    woutT_d = nc.dram_tensor("woutT", [D, D], F32, kind="ExternalInput").ap()
    wrT_d = nc.dram_tensor("wrT", [D, NE], F32, kind="ExternalInput").ap()
    vecs_d = nc.dram_tensor("vecs", [128, 24], F32, kind="ExternalInput").ap()
    ident_d = nc.dram_tensor("ident", [128, 128], F32, kind="ExternalInput").ap()
    w1s_d = nc.dram_tensor("w1s", [FF, D], F32, kind="ExternalInput").ap()
    w2s_d = nc.dram_tensor("w2s", [D, FF], F32, kind="ExternalInput").ap()
    we1_d = nc.dram_tensor("we1", [NE, FF, D], F32, kind="ExternalInput").ap()
    we2_d = nc.dram_tensor("we2", [NE, D, FF], F32, kind="ExternalInput").ap()
    outT_d = nc.dram_tensor("outT", [NSEQ, D, S], F32, kind="ExternalOutput").ap()
    bf_i = mybir.dt.uint16 if debug else BF16
    xn_d = nc.dram_tensor("xn_i", [NSEQ, D, S], bf_i, kind=ikind).ap().bitcast(BF16)
    h_d = nc.dram_tensor("h_i", [NSEQ, D, S], bf_i, kind=ikind).ap().bitcast(BF16)
    gate_d = nc.dram_tensor("gate_i", [NE, NSEQ * S], F32, kind=ikind).ap()
    dbg_d = nc.dram_tensor("dbg_i", [10, 512], F32, kind=ikind).ap() if debug else None
    dbg2_d = nc.dram_tensor("dbg2_i", [128, 1024], F32, kind=ikind).ap() if debug else None
    q1_d = nc.dram_tensor("q1_i", [9, FF, D], bf_i, kind=ikind).ap().bitcast(BF16)
    q2_d = nc.dram_tensor("q2_i", [9, D, FF], bf_i, kind=ikind).ap().bitcast(BF16)

    def sb(name, shape, dtype, off):
        assert PER <= off and off < PER + 195712, (name, off)
        return nc.alloc_sbuf_tensor_at(name, shape, dtype, offset=off).ap()

    vecs = sb("vecs_s", [128, 24], F32, OFF_VECS)
    wrp = sb("wrp_s", [128, NE, NE], F32, OFF_WRP)
    wrt = sb("wrt_s", [128, NE, NE], F32, OFF_WRT)
    rbias = sb("rbias_s", [NE, 1], F32, OFF_RBIAS)
    ones32 = sb("ones32", [128, 1], F32, OFF_ONES32)
    ones32c = sb("ones32c", [1, 128], F32, OFF_ONES32C)
    onesr = sb("onesr", [1, 128], F32R, OFF_ONESR)
    ident = sb("ident_s", [128, 128], F32, OFF_IDENT)
    sc1 = sb("sc1_s", [128, 9, 16], F32, OFF_SC1)
    sc2 = sb("sc2_s", [128, 9, 8], F32, OFF_SC2)
    wbuf = sb("wbuf", [128, 2, 2048], F32, OFF_WBUF)
    pb = sb("pb", [128, 2, 2048], BF16, OFF_PB)
    nb = sb("nb", [128, 2048], BF16, OFF_NB)
    qsc = sb("qsc", [128, 8], F32, OFF_QSC)
    gatebc = sb("gatebc", [128, 1024], BF16, OFF_GATEBC)
    grow = sb("grow", [1, 1024], F32, OFF_GROW)
    growr = sb("growr", [1, 1024], F32R, OFF_GROWR)

    xbuf = sb("xbuf", [128, 2, 8, 512], F32, U0 + P0_XBUF)
    sqb = sb("sqb", [128, 2, 512], F32, U0 + P0_SQ)
    ubuf = sb("ubuf", [128, 2, 512], F32, U0 + P0_UB)
    u1b = sb("u1b", [128, 512], F32, U0 + P0_U1)
    bct = sb("bct", [128, 4, 512], F32, U0 + P0_BCT)
    xnst = sb("xnst", [128, 2, 8, 512], BF16, U0 + P0_XNST)
    hst = sb("hst", [128, 2, 8, 512], BF16, U0 + P0_HST)
    rows = sb("rows", [1, 8, 512], F32, U0 + P0_ROWS)
    logitT = sb("logitT", [NE, NSEQ * S], F32, U0 + P0_LOGIT)
    probsu = sb("probsu", [128, 8], F32, U0 + P0_SMALL)
    srow = sb("srow", [128, 4], F32, U0 + P0_SMALL + 256)
    gate_tm = sb("gate_tm", [128, 8], F32, U0 + P0_SMALL + 512)
    gtmp = sb("gtmp", [NE, 128], F32, U0 + P0_SMALL + 1024)
    dbg2s = sb("dbg2s", [128, 8], F32, U0 + P0_SMALL + 1536)

    qT = sb("qT_s", [128, 8, 2048], BF16, U0 + P1_QT)
    kT = sb("kT_s", [128, 8, 2048], BF16, U0 + P1_KT)
    vP = sb("vP_s", [128, 16, 1040], BF16, U0 + P1_VP)
    xo = sb("xo_s", [128, 8, 2048], BF16, U0 + P1_XO)
    eb = sb("eb_s", [128, 2, 1024], BF16, U0 + P1_EB)
    bcn = sb("bcn_s", [64, 2, 512], F32, U0 + P1_BCN)
    rrow = sb("rrow_s", [1, 1024], F32, U0 + P1_RROW)
    rrowr = sb("rrowr_s", [1, 1024], F32R, U0 + P1_RROWR)
    wqf32 = sb("wqf32", [128, 2, 8, 128], F32, U0 + P1A_WQF32)
    wqbf = sb("wqbf", [128, 2, 8, 128], BF16, U0 + P1A_WQBF)
    wvf32 = sb("wvf32", [128, 8, 256], F32, U0 + P1V_WVF32)
    wvbf = sb("wvbf", [128, 8, 1024], BF16, U0 + P1V_WVBF)
    wof32 = sb("wof32", [128, 2, 8, 128], F32, U0 + P1C_WOF32)
    wobf = sb("wobf", [128, 2, 8, 128], BF16, U0 + P1C_WOBF)
    xstr = sb("xstr", [128, 2, 2048], F32, U0 + P1C_XSTR)
    outst = sb("outst", [128, 2, 2048], F32, U0 + P1C_OUTST)

    q1T = sb("q1T_s", [128, 8, 2048], BF16, U0 + P2_Q1)
    q2T = sb("q2T_s", [128, 16, 1024], BF16, U0 + P2_Q2)
    h_s = sb("h_s", [128, 8, 2048], BF16, U0 + P2_H)
    g1g = sb("g1g_s", [128, 16, 1024], BF16, U0 + P2_G1)
    acc = sb("acc_s", [128, 8, 1024], F32, U0 + P2_ACC)

    ps1 = nc.alloc_psum_tensor("ps1", [128, 2048], F32).ap()
    ps2 = nc.alloc_psum_tensor("ps2", [128, 2048], F32).ap()

    kb = KB(nc)
    C = kb.c

    def dview(ap3, seq):
        return ap3[seq].rearrange("(c p) t -> p c t", p=128)

    t1 = kb.dma('sync', vecs, vecs_d)
    t2 = kb.dma('sync', wrt, wrT_d.rearrange("(c p) e -> p c e", p=128))
    t3 = kb.dma('sync', ident, ident_d)
    sq_setup = [t1, t2, t3]
    kb.op('gpsimd', lambda e: e.memset(ones32, 1.0), inc='pool')
    kb.op('gpsimd', lambda e: e.memset(ones32c, 1.0), inc='pool')
    pool_ones = C('pool')
    kb.op('vector', lambda e: e.tensor_copy(onesr, ones32c),
          waits=[('pool', pool_ones)], inc='dve')
    for cc in range(NCC):
        kb.op('vector', lambda e, o=wrp[:, cc, :], i=wrt[:, cc, :], g=vecs[:, cc:cc + 1]:
              e.tensor_single_scalar(o, i, g, ALU.mult),
              waits=sq_setup, inc='dve')
    dve_wrp = C('dve')
    if "p0lite" not in phases:
        for cc in range(NCC):
            kb.op('tensor', lambda e, i=wrt[:, cc, :], r=vecs[:, 8 + cc:8 + cc + 1], cc=cc:
                  e.matmul(ps1[0:NE, 1536:1537], i, r, start=(cc == 0), stop=(cc == NCC - 1)),
                  waits=sq_setup, inc='pe')
        pe_rbias = C('pe')
        kb.op('scalar', lambda e: e.copy(rbias, ps1[0:NE, 1536:1537]),
              waits=[('pe', pe_rbias)], inc='act')

    qchunks = []
    for m in range(9):
        w1src = w1s_d if m == 0 else we1_d[m - 1]
        w2src = w2s_d if m == 0 else we2_d[m - 1]
        for rc in range(NFC):
            qchunks.append(('q1', m, rc, w1src[rc * 128:(rc + 1) * 128, :], D))
        for rc in range(NDC):
            qchunks.append(('q2', m, rc, w2src[rc * 128:(rc + 1) * 128, :], FF))
    qst = {'i': 0, 'consumed': {}, 'store': {}}
    qstore_done = {}

    def pump_quant(k):
        if "p2" not in phases:
            return
        for _ in range(k):
            i = qst['i']
            if i >= len(qchunks):
                return
            kind, m, rc, src, Cw = qchunks[i]
            p = i % 2
            w_ap = wbuf[:, p, 0:Cw]
            waits = [('pool', qst['consumed'].get(i - 2, 0))]
            ld = kb.dma('sync', w_ap, src, waits=waits)
            red = qsc[:, 4 * p:4 * p + 1]
            thr = qsc[:, 4 * p + 1:4 * p + 2]
            nthr = qsc[:, 4 * p + 2:4 * p + 3]
            kb.op('vector', lambda e, o=red, s=w_ap: e.reduce_sum(
                o, s, axis=AX.X, apply_absolute_value=True), waits=[ld], inc='dve')
            scdst = sc1[:, m, rc:rc + 1] if kind == 'q1' else sc2[:, m, rc:rc + 1]
            kb.op('vector', lambda e, o=scdst, s=red, c=1.0 / Cw: e.tensor_scalar(
                o, s, c, 1e-5, ALU.mult, ALU.max), waits=[('dve', C('dve'))], inc='dve')
            kb.op('vector', lambda e, o=thr, s=scdst: e.tensor_scalar_mul(o, s, 0.5),
                  waits=[('dve', C('dve'))], inc='dve')
            kb.op('vector', lambda e, o=nthr, s=scdst: e.tensor_scalar_mul(o, s, -0.5),
                  inc='dve')
            dve_thr = C('dve')
            pbx = pb[:, p, 0:Cw]
            kb.op('gpsimd', lambda e, o=pbx, s=w_ap, t=thr: e.tensor_single_scalar(
                o, s, t, ALU.is_gt),
                waits=[('dve', dve_thr), qst['store'].get(i - 2, ('pe', 0))], inc='pool')
            kb.op('gpsimd', lambda e, o=nb[:, 0:Cw], s=w_ap, t=nthr: e.tensor_single_scalar(
                o, s, t, ALU.is_lt), inc='pool')
            kb.op('gpsimd', lambda e, o=pbx, a=pbx, b=nb[:, 0:Cw]: e.tensor_tensor(
                o, a, b, ALU.subtract), inc='pool')
            qst['consumed'][i] = C('pool')
            dst = (q1_d if kind == 'q1' else q2_d)[m][rc * 128:(rc + 1) * 128, :]
            st = kb.dma('sync', dst, pbx, waits=[('pool', qst['consumed'][i])])
            qst['store'][i] = st
            qstore_done.setdefault((kind, m), []).append(st)
            qst['i'] = i + 1

    x_consumed = {}
    st_stage = {}
    xn_toks = {}
    h_toks = {}
    rows_read = {}
    bct_done = {}
    sq_x = {}
    pe_sq2 = {}
    act_sq = {}
    pe_router = {}
    pe_router_c = {}
    act_logit = {}

    def xsrc(tc):
        seq, ts = tc // 4, (tc % 4) * 512
        return dview(xT_d, seq)[:, :, ts:ts + 512]

    sq_x[0] = kb.dma('sync', xbuf[:, 0], xsrc(0))
    for tc in range(8):
        p = tc % 2
        if tc + 1 < 8:
            sq_x[tc + 1] = kb.dma('sync', xbuf[:, (tc + 1) % 2], xsrc(tc + 1),
                                  waits=[('dve', x_consumed.get(tc - 1, 0))])
        for cc in range(NCC):
            sp = cc % 2
            w = [sq_x[tc]]
            if (tc, cc - 2) in pe_sq2:
                w.append(('pe', pe_sq2[(tc, cc - 2)]))
            elif tc > 0:
                w.append(('pe', pe_router[tc - 1]))
            kb.op('scalar', lambda e, o=sqb[:, sp, :], i=xbuf[:, p, cc, :]:
                  e.activation(o, i, AF.Square), waits=w, inc='act')
            act_sq[(tc, cc)] = C('act')
            w1 = [sq_x[tc]]
            if cc == 0:
                w1.append(('dve', rows_read.get(tc - 1, 0)))
                if tc == 0:
                    w1.append(('pool', pool_ones))
            kb.op('tensor', lambda e, r=xbuf[:, p, cc, :], cc=cc: e.matmul(
                ps1[0:1, 0:512], ones32, r, start=(cc == 0), stop=(cc == NCC - 1)),
                waits=w1, inc='pe')
            kb.op('tensor', lambda e, r=sqb[:, sp, :], cc=cc: e.matmul(
                ps1[0:1, 512:1024], ones32, r, start=(cc == 0), stop=(cc == NCC - 1)),
                waits=[('act', act_sq[(tc, cc)])], inc='pe')
            pe_sq2[(tc, cc)] = C('pe')

        pe_stat = C('pe')
        r = lambda k: rows[:, k, :]
        kb.op('vector', lambda e: e.tensor_scalar_mul(r(0), ps1[0:1, 0:512], 1.0 / D),
              waits=[('pe', pe_stat)], inc='dve')
        kb.op('vector', lambda e: e.tensor_scalar_mul(r(1), ps1[0:1, 512:1024], 1.0 / D),
              inc='dve')
        rows_read[tc] = C('dve')
        kb.op('vector', lambda e: e.tensor_mul(r(2), r(0), r(0)),
              waits=[('dve', C('dve'))], inc='dve')
        kb.op('vector', lambda e: e.tensor_sub(r(1), r(1), r(2)),
              waits=[('dve', C('dve'))], inc='dve')
        kb.op('vector', lambda e: e.tensor_scalar_add(r(2), r(1), LN_EPS),
              waits=[('dve', C('dve'))], inc='dve')
        dve_a = C('dve')
        kb.op('scalar', lambda e: e.activation(r(3), r(2), AF.Sqrt),
              waits=[('dve', dve_a)], inc='act')
        act_s1 = C('act')
        kb.op('vector', lambda e: e.reciprocal(r(4), r(3)), waits=[('act', act_s1)], inc='dve')
        kb.op('vector', lambda e: e.reciprocal(r(5), r(2)), inc='dve')
        kb.op('vector', lambda e: e.tensor_mul(r(5), r(5), r(1)),
              waits=[('dve', C('dve'))], inc='dve')
        kb.op('vector', lambda e: e.tensor_scalar_add(r(5), r(5), LN_EPS),
              waits=[('dve', C('dve'))], inc='dve')
        dve_b = C('dve')
        kb.op('scalar', lambda e: e.activation(r(6), r(5), AF.Sqrt),
              waits=[('dve', dve_b)], inc='act')
        act_s2 = C('act')
        kb.op('vector', lambda e: e.reciprocal(r(7), r(6)), waits=[('act', act_s2)], inc='dve')
        kb.op('vector', lambda e: e.tensor_mul(r(7), r(7), r(4)),
              waits=[('dve', C('dve'))], inc='dve')
        kb.op('vector', lambda e: e.tensor_mul(r(5), r(0), r(4)),
              waits=[('dve', C('dve'))], inc='dve')
        kb.op('vector', lambda e: e.tensor_mul(r(6), r(0), r(7)),
              waits=[('dve', C('dve'))], inc='dve')
        dve_rows = C('dve')
        for k, ridx in enumerate([4, 5, 7, 6]):
            w = [('dve', dve_rows)]
            if tc > 0:
                w.append(('act', bct_done[tc - 1]))
            else:
                w.append(('pool', pool_ones))
            kb.op('tensor', lambda e, o=ps2[:, k * 512:(k + 1) * 512], rr=rows[:, ridx, :]:
                  e.matmul(o, ones32c, rr, start=True, stop=True), waits=w, inc='pe')
        pe_bc = C('pe')
        for k in range(4):
            kb.op('scalar', lambda e, o=bct[:, k, :], i=ps2[:, k * 512:(k + 1) * 512]:
                  e.copy(o, i), waits=[('pe', pe_bc)], inc='act')
        bct_done[tc] = C('act')
        for cc in range(NCC):
            up = cc % 2
            w = [('act', bct_done[tc])]
            if cc == 0:
                w.extend(st_stage.get(tc - 2, []))
            kb.op('vector', lambda e, i=xbuf[:, p, cc, :]: e.tensor_mul(u1b, i, bct[:, 0, :]),
                  waits=w, inc='dve')
            kb.op('vector', lambda e, o=xnst[:, p, cc, :]: e.tensor_sub(o, u1b, bct[:, 1, :]),
                  inc='dve')
            w2 = []
            if cc >= 2:
                w2.append(('pe', pe_router_c[(tc, cc - 2)]))
            kb.op('vector', lambda e, o=ubuf[:, up, :], i=xbuf[:, p, cc, :]:
                  e.tensor_mul(o, i, bct[:, 2, :]), waits=w2, inc='dve')
            kb.op('vector', lambda e, o=ubuf[:, up, :]: e.tensor_sub(o, o, bct[:, 3, :]),
                  inc='dve')
            kb.op('vector', lambda e, o=hst[:, p, cc, :], i=ubuf[:, up, :],
                  g=vecs[:, cc:cc + 1], b=vecs[:, 8 + cc:8 + cc + 1]:
                  e.tensor_scalar(o, i, g, b, ALU.mult, ALU.add), inc='dve')
            dve_u = C('dve')
            w3 = [('dve', dve_u), ('dve', dve_wrp)]
            if cc == 0 and tc > 0:
                w3.append(('act', act_logit[tc - 1]))
            kb.op('tensor', lambda e, i=wrp[:, cc, :], rr=ubuf[:, up, :], cc=cc:
                  e.matmul(ps1[0:NE, 1024:1536], i, rr, start=(cc == 0), stop=(cc == NCC - 1)),
                  waits=w3, inc='pe')
            pe_router_c[(tc, cc)] = C('pe')
        pe_router[tc] = C('pe')
        x_consumed[tc] = C('dve')
        dve_norm = C('dve')
        kb.op('scalar', lambda e, o=logitT[:, tc * 512:(tc + 1) * 512]:
              e.copy(o, ps1[0:NE, 1024:1536]), waits=[('pe', pe_router[tc])], inc='act')
        act_logit[tc] = C('act')
        seq, ts = tc // 4, (tc % 4) * 512
        t_xn = kb.dma('sync', dview(xn_d, seq)[:, :, ts:ts + 512], xnst[:, p],
                      waits=[('dve', dve_norm)])
        t_h = kb.dma('sync', dview(h_d, seq)[:, :, ts:ts + 512], hst[:, p],
                     waits=[('dve', dve_norm)])
        st_stage[tc] = [t_xn, t_h]
        xn_toks.setdefault(seq, []).append(t_xn)
        h_toks.setdefault(seq, []).append(t_h)
        pump_quant(3)

    kb.op('vector', lambda e: e.tensor_single_scalar(logitT, logitT, rbias, ALU.add),
          waits=[('act', C('act'))], inc='dve')
    dve_lbias = C('dve')
    pe_T2 = {}
    act_exp0 = {}
    act_gtmp = {}
    dve_g = {}
    gate_store = 0
    gstore_sq = {}
    gate_toks = []
    for tb in range(32):
        w = [('dve', dve_lbias), ('act', act_exp0.get(tb - 1, 0))]
        kb.op('tensor', lambda e, i=logitT[:, tb * 128:(tb + 1) * 128]:
              e.matmul(ps1[0:128, 0:8], i, ident[0:NE, 0:NE], start=True, stop=True),
              waits=w, inc='pe')
        pe_t1 = C('pe')
        w = [('pe', pe_t1), ('dve', dve_g.get(tb - 1, 0))]
        kb.op('scalar', lambda e: e.activation(probsu, ps1[0:128, 0:8], AF.Exp),
              waits=w, inc='act')
        act_exp0[tb] = C('act')
        kb.op('vector', lambda e: e.reduce_sum(srow[:, 0:1], probsu, axis=AX.X),
              waits=[('act', act_exp0[tb])], inc='dve')
        kb.op('vector', lambda e: e.reciprocal(srow[:, 1:2], srow[:, 0:1]),
              waits=[('dve', C('dve'))], inc='dve')
        kb.op('vector', lambda e: e.reduce_max(srow[:, 2:3], probsu, axis=AX.X),
              waits=[('dve', C('dve'))], inc='dve')
        kb.op('vector', lambda e: e.tensor_mul(srow[:, 3:4], srow[:, 2:3], srow[:, 1:2]),
              waits=[('dve', C('dve'))], inc='dve')
        kb.op('vector', lambda e: e.tensor_scalar(gate_tm, probsu, srow[:, 2:3], srow[:, 3:4],
                                                  ALU.is_equal, ALU.mult),
              waits=[('pe', pe_T2.get(tb - 1, 0)), ('dve', C('dve'))], inc='dve')
        dve_g[tb] = C('dve')
        w = [('dve', dve_g[tb]), ('act', act_gtmp.get(tb - 1, 0))]
        kb.op('tensor', lambda e: e.matmul(ps1[0:NE, 512:640], gate_tm, ident),
              waits=w, inc='pe')
        pe_T2[tb] = C('pe')
        w = [('pe', pe_T2[tb]), gstore_sq.get(tb - 1, ('pe', 0))]
        kb.op('scalar', lambda e: e.copy(gtmp, ps1[0:NE, 512:640]), waits=w, inc='act')
        act_gtmp[tb] = C('act')
        gate_store = kb.dma('sync', gate_d[:, tb * 128:(tb + 1) * 128], gtmp,
                            waits=[('act', act_gtmp[tb])])
        gstore_sq[tb] = gate_store
        gate_toks.append(gate_store)
    fin_pe = pe_T2[31]
    fin_act = act_gtmp[31]

    B_done = {}
    C_store_toks = {0: [], 1: []}
    C_mm_done = {}
    C_stt_done = {}
    C_store_done = {}
    A_qk_evac = {}
    A_v_evac = {}
    norm_dve_last = 0

    for seq in (range(NSEQ) if "p1" in phases else []):
        w = kb.maxtok(xn_toks[seq])
        if seq == 0:
            w += [('pe', fin_pe), ('act', fin_act)] + kb.maxtok(gate_toks)
        else:
            w.append(('pe', C_mm_done[seq - 1]))
        sq_xn = kb.dma('sync', xo, dview(xn_d, seq), waits=w)
        pool_cast = {}
        pe_grp = {}
        dve_evac = {}
        for j in range(16):
            jp = j % 2
            fbase = j * 128
            w = [('pool', pool_cast.get(j - 2, 0))]
            if j < 2 and seq > 0:
                w.append(('pe', C_mm_done[seq - 1]))
            ldw = kb.dma('sync', wqf32[:, jp],
                         wqkvT_d[:, fbase:fbase + 128].rearrange("(c p) f -> p c f", p=128),
                         waits=w)
            wc = [ldw, ('pe', pe_grp.get(j - 2, 0))]
            if j < 2 and seq > 0:
                wc.append(('pe', C_mm_done[seq - 1]))
            kb.op('gpsimd', lambda e, o=wqbf[:, jp], i=wqf32[:, jp]: e.tensor_copy(o, i),
                  waits=wc, inc='pool')
            pool_cast[j] = C('pool')
            psq = ps1 if jp == 0 else ps2
            for cc in range(NCC):
                for tq in range(4):
                    w = []
                    if cc == 0 and tq == 0:
                        w = [('pool', pool_cast[j]), sq_xn,
                             ('dve', dve_evac.get(j - 2, 0))]
                        if j < 2:
                            if seq == 0:
                                w.append(('act', fin_act))
                            else:
                                w.append(('dve', C_stt_done[seq - 1]))
                    last = (cc == NCC - 1 and tq == 3)
                    kb.op('tensor', lambda e, o=psq[:, tq * 512:(tq + 1) * 512],
                          i=wqbf[:, jp, cc, :], rr=xo[:, cc, tq * 512:(tq + 1) * 512], cc=cc:
                          e.matmul(o, i, rr, start=(cc == 0), stop=(cc == NCC - 1)),
                          waits=w, inc='pe' if last else None)
            pe_grp[j] = C('pe')
            dst = qT[:, j, :] if j < 8 else kT[:, j - 8, :]
            w = [('pe', pe_grp[j])]
            if seq > 0 and j >= 8 and j < 10:
                w.extend(kb.maxtok(C_store_toks[seq - 1]))
            kb.op('vector', lambda e, o=dst, i=psq: e.tensor_copy(o, i), waits=w, inc='dve')
            dve_evac[j] = C('dve')
            pump_quant(1)
        A_qk_evac[seq] = C('dve')
        pool_vc = {}
        for rnd in range(4):
            w = [('pool', pool_vc.get(rnd - 1, 0))]
            if rnd == 0:
                w.append(('pool', pool_cast[15]))
            if rnd == 0 and seq == 0:
                w += [('pe', fin_pe), ('act', fin_act)]
            if rnd == 0 and seq > 0:
                w.append(('pe', C_mm_done[seq - 1]))
            ldv = kb.dma('sync', wvf32,
                         wqkvT_d[:, 2048 + rnd * 256:2048 + (rnd + 1) * 256]
                         .rearrange("(c p) f -> p c f", p=128), waits=w)
            wc = [ldv]
            if rnd == 0:
                wc.append(('pe', pe_grp[15]))
            if rnd == 0 and seq > 0:
                wc.append(('pe', C_mm_done[seq - 1]))
            kb.op('gpsimd', lambda e, o=wvbf[:, :, rnd * 256:(rnd + 1) * 256], i=wvf32:
                  e.tensor_copy(o, i), waits=wc, inc='pool')
            pool_vc[rnd] = C('pool')
        pool_vcast = C('pool')
        w = kb.maxtok(xn_toks[1] + h_toks[1]) + [('pe', fin_pe)] if seq == 0 else [('pe', B_done[seq - 1])]
        kb.op('gpsimd', lambda e: e.memset(
            vP.rearrange("p k (h x) -> p k h x", h=16)[:, :, :, 64:65], 1.0),
            waits=w, inc='pool')
        pool_vones = C('pool')
        act_vevac = {}
        for tb in range(16):
            tp = tb % 2
            psv = ps1[:, tp * 1024:(tp + 1) * 1024]
            for cc in range(NCC):
                for g in range(2):
                    w = []
                    if cc == 0 and g == 0:
                        w = [('pool', pool_vcast), ('act', act_vevac.get(tb - 2, 0)),
                             ('dve', A_qk_evac[seq])]
                    last = (cc == NCC - 1 and g == 1)
                    kb.op('tensor', lambda e, o=psv[:, g * 512:(g + 1) * 512],
                          i=xo[:, cc, tb * 128:(tb + 1) * 128],
                          rr=wvbf[:, cc, g * 512:(g + 1) * 512], cc=cc:
                          e.matmul(o, i, rr, start=(cc == 0), stop=(cc == NCC - 1)),
                          waits=w, inc='pe' if last else None)
            pe_v = C('pe')
            kb.op('scalar', lambda e,
                  o=vP[:, tb, :].rearrange("p (h x) -> p h x", h=16)[:, :, 0:64],
                  i=psv.rearrange("p (h d) -> p h d", h=16): e.copy(o, i),
                  waits=[('pe', pe_v), ('pool', pool_vones)], inc='act')
            act_vevac[tb] = C('act')
        A_v_evac[seq] = C('act')
        pump_quant(2)

        skipB = "p1nob" in phases
        exp_done = {}
        pvb_done = {}
        prev_exp_tail = 0
        bcn_copy_prev = 0
        for hp in (range(HP) if not skipB else []):
            hA, hB = 2 * hp, 2 * hp + 1
            for qb in range(NQB):
                qs = qb * QB
                for kbk in range(NKB):
                    par = kbk % 2
                    ks = kbk * 128
                    spair = ps1[:, par * 1024:(par + 1) * 1024]
                    if kbk < 2:
                        w = [('act', prev_exp_tail)]
                        if kbk == 0:
                            w += [('dve', A_qk_evac[seq]), ('dve', norm_dve_last),
                                  ('act', A_v_evac[seq])]
                    else:
                        w = [('act', exp_done[(hp, qb, kbk - 2)])]
                    kb.op('tensor', lambda e, o=spair[:, 0:512], i=kT[0:64, hp, ks:ks + 128],
                          rr=qT[0:64, hp, qs:qs + 512]:
                          e.matmul(o, i, rr, start=True, stop=True), waits=w, inc=None)
                    kb.op('tensor', lambda e, o=spair[:, 512:1024],
                          i=kT[64:128, hp, ks:ks + 128], rr=qT[64:128, hp, qs:qs + 512]:
                          e.matmul(o, i, rr, start=True, stop=True, tile_position=(64, 0)),
                          inc='pe')
                    pe_sb = C('pe')
                    kb.op('scalar', lambda e, o=eb[:, par, :], i=spair: e.activation(
                        o, i, AF.Exp, scale=0.125), waits=[('pe', pe_sb)], inc='act')
                    exp_done[(hp, qb, kbk)] = C('act')
                    w = [('act', exp_done[(hp, qb, kbk)])]
                    if kbk == 0:
                        w.append(('act', A_v_evac[seq]))
                    kb.op('tensor', lambda e, i=vP[:, kbk, hA * 65:hA * 65 + 65],
                          rr=eb[:, par, 0:512], kbk=kbk: e.matmul(
                              ps2[0:65, 0:512], i, rr, start=(kbk == 0), stop=(kbk == NKB - 1)),
                          waits=w, inc=None)
                    kb.op('tensor', lambda e, i=vP[:, kbk, hB * 65:hB * 65 + 65],
                          rr=eb[:, par, 512:1024], kbk=kbk: e.matmul(
                              ps2[0:65, 512:1024], i, rr, start=(kbk == 0), stop=(kbk == NKB - 1)),
                          inc='pe')
                    pvb_done[(hp, qb, kbk)] = C('pe')
                prev_exp_tail = exp_done[(hp, qb, NKB - 1)]
                pe_pv = pvb_done[(hp, qb, NKB - 1)]
                kb.op('vector', lambda e: e.reciprocal(rrow[:, 0:512], ps2[64:65, 0:512]),
                      waits=[('pe', pe_pv)], inc='dve')
                kb.op('vector', lambda e: e.reciprocal(rrow[:, 512:1024], ps2[64:65, 512:1024]),
                      inc='dve')
                kb.op('vector', lambda e: e.tensor_copy(rrowr, rrow),
                      waits=[('dve', C('dve'))], inc='dve')
                dve_rr = C('dve')
                kb.op('tensor', lambda e: e.matmul(ps2[0:64, 1024:1536], onesr[:, 0:64],
                                                   rrowr[:, 0:512], start=True, stop=True),
                      waits=[('dve', dve_rr), ('act', bcn_copy_prev)], inc=None)
                kb.op('tensor', lambda e: e.matmul(ps2[0:64, 1536:2048], onesr[:, 0:64],
                                                   rrowr[:, 512:1024], start=True, stop=True),
                      inc='pe')
                pe_bcn = C('pe')
                kb.op('scalar', lambda e: e.copy(bcn.rearrange("p a x -> p (a x)"),
                                                 ps2[0:64, 1024:2048]),
                      waits=[('pe', pe_bcn)], inc='act')
                bcn_copy_prev = C('act')
                kb.op('vector', lambda e, o=xo[0:64, hp, qs:qs + 512]: e.tensor_mul(
                    o, ps2[0:64, 0:512], bcn[:, 0, :]),
                    waits=[('act', bcn_copy_prev)], inc='dve')
                kb.op('vector', lambda e, o=xo[64:128, hp, qs:qs + 512]: e.tensor_mul(
                    o, ps2[0:64, 512:1024], bcn[:, 1, :]), inc='dve')
                norm_dve_last = C('dve')
                pump_quant(2)
        B_done[seq] = C('pe')

        skipC = "p1noc" in phases
        pool_oc = {}
        pe_ogrp = {}
        dve_stt = {}
        st_out = ('pe', 0)
        st_toks = {}
        for fc in (range(NCC) if not skipC else []):
            fp = fc % 2
            w = [('pool', pool_oc.get(fc - 2, 0))]
            if fc < 2:
                w += [('pe', B_done[seq]), ('dve', norm_dve_last)]
            ldo = kb.dma('sync', wof32[:, fp],
                         woutT_d[:, fc * 128:(fc + 1) * 128].rearrange("(c p) f -> p c f", p=128),
                         waits=w)
            wc = [ldo, ('pe', pe_ogrp.get(fc - 2, 0))]
            if fc < 2:
                wc.append(('pe', B_done[seq]))
            kb.op('gpsimd', lambda e, o=wobf[:, fp], i=wof32[:, fp]: e.tensor_copy(o, i),
                  waits=wc, inc='pool')
            pool_oc[fc] = C('pool')
            w = [('dve', dve_stt.get(fc - 2, 0)), ('pe', B_done[seq])]
            if seq > 0 and fc < 2:
                w.extend(kb.maxtok(C_store_toks[seq - 1]))
            sq_xt = kb.dma('sync', xstr[:, fp, :], xT_d[seq, fc * 128:(fc + 1) * 128, :],
                           waits=w)
            psq = ps1 if fp == 0 else ps2
            for cc in range(NCC):
                for tq in range(4):
                    w = []
                    if cc == 0 and tq == 0:
                        w = [('pool', pool_oc[fc]), ('dve', norm_dve_last),
                             ('dve', dve_stt.get(fc - 2, 0))]
                    last = (cc == NCC - 1 and tq == 3)
                    kb.op('tensor', lambda e, o=psq[:, tq * 512:(tq + 1) * 512],
                          i=wobf[:, fp, cc, :], rr=xo[:, cc, tq * 512:(tq + 1) * 512], cc=cc:
                          e.matmul(o, i, rr, start=(cc == 0), stop=(cc == NCC - 1)),
                          waits=w, inc='pe' if last else None)
            pe_ogrp[fc] = C('pe')
            kb.op('vector', lambda e, o=outst[:, fp, :], i=psq, b=vecs[:, 16 + fc:16 + fc + 1],
                  xx=xstr[:, fp, :]: e.scalar_tensor_tensor(o, i, b, xx, ALU.add, ALU.add),
                  waits=[('pe', pe_ogrp[fc]), sq_xt, st_toks.get(fc - 2, ('pe', 0))],
                  inc='dve')
            dve_stt[fc] = C('dve')
            st_out = kb.dma('sync', outT_d[seq, fc * 128:(fc + 1) * 128, :], outst[:, fp, :],
                            waits=[('dve', dve_stt[fc])])
            st_toks[fc] = st_out
            pump_quant(1)
        C_mm_done[seq] = C('pe')
        C_stt_done[seq] = C('dve')
        C_store_done[seq] = st_out if not skipC else ('pe', 0)
        C_store_toks[seq] = list(st_toks.values())

    pump_quant(len(qchunks))
    pool_qdone = C('pool')

    def xbar_q1(m, waits):
        toks = []
        for cc in range(NCC):
            toks.append(kb.dma('scalar', q1T[:, cc, :], q1_d[m][:, cc * 128:(cc + 1) * 128],
                               waits=waits if cc == 0 else (), transpose=True))
        return kb.maxtok(toks)

    def xbar_q2(m, waits):
        toks = []
        for fc in range(NFC):
            toks.append(kb.dma('scalar', q2T[:, fc, :], q2_d[m][:, fc * 128:(fc + 1) * 128],
                               waits=waits if fc == 0 else (), transpose=True))
        return kb.maxtok(toks)

    if "p2" in phases:
        cq_q1 = xbar_q1(0, kb.maxtok(qstore_done[('q1', 0)]) + [('dve', C_stt_done[1]),
                            ('pe', B_done[1])])
        cq_q2 = xbar_q2(0, kb.maxtok(qstore_done[('q2', 0)] + C_store_toks[1]) +
                        [('pe', B_done[1])])

    iters = [(s, t) for s in range(NSEQ) for t in range(2)] if "p2" in phases else []
    prev_acc_store = ('pe', 0)
    sq_h = 0
    h_users_pe = 0
    dve_acc_last = 0
    act_silu = {}
    dve_gate = {}
    pe_w1_last = {}
    pe_w2_last = {}
    dve_stt2 = {}
    dve_gr_prev = 0
    act_gbc = 0
    prev_silu_tail = 0
    prev_stt2_tail = 0
    final_st = 0
    for it, (seq, tt) in enumerate(iters):
        if tt == 0:
            sq_h = kb.dma('sync', h_s, dview(h_d, seq),
                          waits=kb.maxtok(h_toks[seq]) + [('pe', B_done[1]),
                                                          ('pe', h_users_pe)])
        sq_acc = kb.dma('sync', acc, dview(outT_d, seq)[:, :, tt * TT:(tt + 1) * TT],
                        waits=kb.maxtok(C_store_toks[seq]) + [('dve', dve_acc_last),
                               ('pe', C_mm_done[1]), prev_acc_store])
        for m in range(9):
            if m > 0:
                w = kb.maxtok(gate_toks) + [('dve', dve_gr_prev)]
                if it == 0 and m == 1:
                    w.append(('pool', pool_qdone))
                sq_g = kb.dma('sync', grow,
                              gate_d[m - 1:m, seq * S + tt * TT:seq * S + (tt + 1) * TT],
                              waits=w)
                kb.op('vector', lambda e: e.tensor_copy(growr, grow),
                      waits=[sq_g], inc='dve')
                dve_gr_prev = C('dve')
                kb.op('tensor', lambda e: e.matmul(ps1[:, 0:512], onesr, growr[:, 0:512],
                                                   start=True, stop=True),
                      waits=[('dve', dve_gr_prev), ('act', prev_silu_tail)], inc=None)
                kb.op('tensor', lambda e: e.matmul(ps1[:, 512:1024], onesr, growr[:, 512:1024],
                                                   start=True, stop=True), inc='pe')
                pe_gbc = C('pe')
                w = [('pe', pe_gbc), ('dve', dve_gate.get((it, m - 1, NFC - 1), 0))]
                if it == 0 and m == 1:
                    w.append(('pool', pool_qdone))
                kb.op('scalar', lambda e: e.copy(gatebc, ps1[:, 0:1024]), waits=w, inc='act')
                act_gbc = C('act')
            for fc in range(NFC):
                fp = fc % 2
                psq = ps1[:, fp * 1024:(fp + 1) * 1024]
                for cc in range(NCC):
                    for tq in range(2):
                        w = []
                        if cc == 0 and tq == 0:
                            w = list(cq_q1) + [sq_h]
                            if fc >= 2:
                                w.append(('act', act_silu[(it, m, fc - 2)]))
                            else:
                                w.append(('act', max(prev_silu_tail, act_gbc)))
                        last = (cc == NCC - 1 and tq == 1)
                        kb.op('tensor', lambda e, o=psq[:, tq * 512:(tq + 1) * 512],
                              i=q1T[:, cc, fc * 128:(fc + 1) * 128],
                              rr=h_s[:, cc, tt * TT + tq * 512:tt * TT + (tq + 1) * 512], cc=cc:
                              e.matmul(o, i, rr, start=(cc == 0), stop=(cc == NCC - 1)),
                              waits=w, inc='pe' if last else None)
                pe_g = C('pe')
                w = [('pe', pe_g)]
                if m == 0:
                    w.append(('pe', pe_w2_last.get((it - 1, 8), 0)))
                else:
                    w.append(('pe', pe_w2_last.get((it, m - 1), 0)))
                kb.op('scalar', lambda e, o=g1g[:, fc, :], i=psq, s=sc1[:, m, fc:fc + 1]:
                      e.activation(o, i, AF.Silu, scale=s), waits=w, inc='act')
                act_silu[(it, m, fc)] = C('act')
                if m > 0:
                    kb.op('vector', lambda e, o=g1g[:, fc, :]: e.tensor_mul(o, o, gatebc),
                          waits=[('act', act_silu[(it, m, fc)])], inc='dve')
                dve_gate[(it, m, fc)] = C('dve')
            pe_w1_last[(it, m)] = C('pe')
            prev_silu_tail = act_silu[(it, m, NFC - 1)]
            if not (it == len(iters) - 1 and m == 8):
                nm = m + 1 if m + 1 <= 8 else 0
                cq_q1 = xbar_q1(nm, kb.maxtok(qstore_done[('q1', nm)]) +
                                [('pe', pe_w1_last[(it, m)])])
            for dc in range(NDC):
                dp = dc % 2
                psq2 = ps2[:, dp * 1024:(dp + 1) * 1024]
                for fc in range(NFC):
                    for tq in range(2):
                        w = []
                        if fc == 0 and tq == 0:
                            w = list(cq_q2)
                            if dc >= 2:
                                w.append(('dve', dve_stt2[(it, m, dc - 2)]))
                            else:
                                w.append(('dve', prev_stt2_tail))
                        if tq == 0:
                            if m > 0:
                                w.append(('dve', dve_gate[(it, m, fc)]))
                            else:
                                w.append(('act', act_silu[(it, m, fc)]))
                        last = (fc == NFC - 1 and tq == 1)
                        kb.op('tensor', lambda e, o=psq2[:, tq * 512:(tq + 1) * 512],
                              i=q2T[:, fc, dc * 128:(dc + 1) * 128],
                              rr=g1g[:, fc, tq * 512:(tq + 1) * 512], fc=fc:
                              e.matmul(o, i, rr, start=(fc == 0), stop=(fc == NFC - 1)),
                              waits=w, inc='pe' if last else None)
                pe_g2 = C('pe')
                kb.op('vector', lambda e, o=acc[:, dc, :], i=psq2, s=sc2[:, m, dc:dc + 1],
                      a=acc[:, dc, :]: e.scalar_tensor_tensor(o, i, s, a, ALU.mult, ALU.add),
                      waits=[('pe', pe_g2), sq_acc], inc='dve')
                dve_stt2[(it, m, dc)] = C('dve')
            pe_w2_last[(it, m)] = C('pe')
            prev_stt2_tail = dve_stt2[(it, m, NDC - 1)]
            if not (it == len(iters) - 1 and m == 8):
                nm = m + 1 if m + 1 <= 8 else 0
                cq_q2 = xbar_q2(nm, kb.maxtok(qstore_done[('q2', nm)]) +
                                [('pe', pe_w2_last[(it, m)])])
        dve_acc_last = C('dve')
        h_users_pe = C('pe')
        final_st = kb.dma('sync', dview(outT_d, seq)[:, :, tt * TT:(tt + 1) * TT], acc,
                          waits=[('dve', dve_acc_last)])
        prev_acc_store = final_st

    kb.op('sync', lambda e: e.nop(), waits=kb.all_dma() + [('act', kb.c('act')),
                                            ('pe', kb.c('pe')), ('dve', kb.c('dve')),
                                            ('pool', kb.c('pool'))])

    import contextlib
    with contextlib.ExitStack() as ctx:
        sems = {}
        names = ['pe', 'act', 'dve', 'pool']
        names += ['sq%d' % i for i in range(KB.NQSEM)]
        names += ['cq%d' % i for i in range(KB.NQSEM)]
        for name in names:
            sems[name] = ctx.enter_context(nc.semaphore(name))
        kb.build(sems)
    return nc


_NC_CACHE = None


def kernel(**inputs):
    global _NC_CACHE
    x = np.ascontiguousarray(np.asarray(inputs["L0_attn_x"], dtype=np.float32))
    w_qkv = np.asarray(inputs["L0_attn_w_qkv"], dtype=np.float32)
    w_out = np.asarray(inputs["L0_attn_w_out"], dtype=np.float32)
    b_out = np.asarray(inputs["L0_attn_b_out"], dtype=np.float32)
    w1s = np.ascontiguousarray(np.asarray(inputs["L0_ffn_shared_w1"], dtype=np.float32))
    w2s = np.ascontiguousarray(np.asarray(inputs["L0_ffn_shared_w2"], dtype=np.float32))
    we1 = np.ascontiguousarray(np.asarray(inputs["L0_ffn_expert_w1"], dtype=np.float32))
    we2 = np.ascontiguousarray(np.asarray(inputs["L0_ffn_expert_w2"], dtype=np.float32))
    wr = np.asarray(inputs["L0_ffn_router_w"], dtype=np.float32)
    gamma = np.asarray(inputs["L0_ffn_ln_gamma"], dtype=np.float32)
    beta = np.asarray(inputs["L0_ffn_ln_beta"], dtype=np.float32)

    wqkvT = np.ascontiguousarray(w_qkv.T)
    woutT = np.ascontiguousarray(w_out.T)
    wrT = np.ascontiguousarray(wr.T)
    vecs = np.zeros((128, 24), np.float32)
    vecs[:, 0:8] = gamma.reshape(8, 128).T
    vecs[:, 8:16] = beta.reshape(8, 128).T
    vecs[:, 16:24] = b_out.reshape(8, 128).T
    ident = np.eye(128, dtype=np.float32)

    if _NC_CACHE is None:
        _NC_CACHE = build_nc()
    nc = _NC_CACHE

    in_maps = []
    for k in range(NCORES):
        xk = np.ascontiguousarray(x[2 * k:2 * k + 2].transpose(0, 2, 1))
        in_maps.append({
            "xT": xk, "wqkvT": wqkvT, "woutT": woutT, "wrT": wrT,
            "vecs": vecs, "ident": ident,
            "w1s": w1s, "w2s": w2s, "we1": we1, "we2": we2,
        })
    res = run_bass_kernel_spmd(nc, in_maps, core_ids=list(range(NCORES)))
    outs = []
    for k in range(NCORES):
        o = res.results[k]["outT"]
        outs.append(np.asarray(o).transpose(0, 2, 1))
    return np.ascontiguousarray(np.concatenate(outs, axis=0))


# version 03
# speedup vs baseline: 1.5278x; 1.5278x over previous
import numpy as np
import ml_dtypes
import concourse.bass as bass
import concourse.mybir as mybir
from concourse.bass_utils import run_bass_kernel_spmd

F32 = mybir.dt.float32
F32R = mybir.dt.float32r
BF16 = mybir.dt.bfloat16
AF = mybir.ActivationFunctionType
ALU = mybir.AluOpType
AX = mybir.AxisListType

NCORES = 8
NSEQ = 2
S = 2048
D = 1024
NCC = 8
H = 16
HP = 8
HD = 64
QB = 512
NQB = S // QB
NKB = S // 128
NE = 8
FF = 2048
NFC = FF // 128
NDC = D // 128
TT = 1024
LN_EPS = 1e-5

TOP = 212352
PER = 16640
OFF_VECS = PER
OFF_WRP = PER + 128
OFF_WRT = OFF_WRP + 256
OFF_RBIAS = OFF_WRT + 256
OFF_ONES32 = OFF_RBIAS + 32
OFF_ONES32C = OFF_ONES32 + 32
OFF_ONESR = OFF_ONES32C + 512
OFF_IDENT = OFF_ONESR + 512
OFF_SC1 = OFF_IDENT + 512
OFF_SC2 = OFF_SC1 + 576
PERSIST_END = OFF_SC2 + 320
QR = PERSIST_END
OFF_WBUF = QR
OFF_PB = QR + 16384
OFF_NB = OFF_PB + 8192
OFF_QSC = OFF_NB + 4096
QUANT_END = OFF_QSC + 64
OFF_GATEBC = QR
OFF_GROW = QR + 2048
OFF_GROWR = QR + 6144
U0 = ((QUANT_END + 63) // 64) * 64
assert U0 + 163840 <= TOP, (U0, TOP)

P0_XBUF = 0
P0_SQ = 32768
P0_UB = 36864
P0_U1 = 40960
P0_BCT = 43008
P0_XNST = 51200
P0_HST = 67584
P0_ROWS = 83968
P0_LOGIT = 100352
P0_SMALL = 116736

P1_QT = 0
P1_KT = 32768
P1_VP = 65536
P1_XO = 98816
P1_EB = 131584
P1_BCN = 135680
P1_RROW = 139776
P1_RROWR = 143872
P1A_WQF32 = 131584
P1A_WQBF = 139776
P1V_WVF32 = 131584
P1V_WVBF = 139776
P1C_WOF32 = 131584
P1C_WOBF = 139776
P1C_XSTR = 0
P1C_OUTST = 32768

P2_Q1 = 0
P2_Q2 = 32768
P2_H = 65536
P2_G1 = 98304
P2_ACC = 131072


class KB:
    ENGINES = ("sync", "scalar", "vector", "tensor", "gpsimd")

    def __init__(self, nc):
        self.nc = nc
        self.ops = []
        self.cnt = {}

    ENGSEM = {'vector': 'dve', 'scalar': 'act', 'gpsimd': 'pool'}

    def op(self, engine, fn, waits=(), inc=None, n=1):
        ws = []
        for s, v in waits:
            if v <= 0:
                continue
            assert v <= self.cnt.get(s, 0), (engine, s, v, self.cnt.get(s, 0))
            ws.append((s, v))
        es = self.ENGSEM.get(engine)
        if es is not None:
            prev = self.cnt.get(es, 0)
            if prev > 0 and not any(s == es and v >= prev for (s, v) in ws):
                ws.append((es, prev))
            if inc is None:
                inc, n = es, 1
        self.ops.append((engine, fn, tuple(ws), inc, n))
        if inc is not None:
            self.cnt[inc] = self.cnt.get(inc, 0) + n
            return self.cnt[inc]
        return None

    NQSEM = 6

    def dma(self, q, out, in_, waits=(), sem=None, transpose=False):
        if not hasattr(self, 'qrr'):
            self.qrr = {}
        i = self.qrr.get(q, 0)
        self.qrr[q] = (i + 1) % self.NQSEM
        snm = ('sq%d' if q == 'sync' else 'cq%d') % i
        w = list(waits)
        prev = self.cnt.get(snm, 0)
        if prev > 0:
            w.append((snm, prev))
        if transpose:
            fn = lambda e, o=out, i=in_: e.dma_start_transpose(o, i)
        else:
            fn = lambda e, o=out, i=in_: e.dma_start(out=o, in_=i)
        v = self.op(q, fn, w, inc=snm, n=16)
        return (snm, v)

    @staticmethod
    def maxtok(toks):
        best = {}
        for (nm, v) in toks:
            if v > best.get(nm, 0):
                best[nm] = v
        return list(best.items())

    def all_dma(self):
        toks = []
        for i in range(self.NQSEM):
            for pre in ('sq', 'cq'):
                nm = '%s%d' % (pre, i)
                if self.cnt.get(nm, 0) > 0:
                    toks.append((nm, self.cnt[nm]))
        return toks

    def c(self, sem):
        return self.cnt.get(sem, 0)

    def build(self, sems):
        used = {e for (e, *_r) in self.ops}
        with self.nc.Block() as blk:
            for en in self.ENGINES:
                if en not in used:
                    continue

                def mk(en):
                    def body(eng):
                        for (e, fn, waits, inc, n) in self.ops:
                            if e != en:
                                continue
                            for (s, v) in waits:
                                eng.wait_ge(sems[s], v)
                            ins = fn(eng)
                            if inc is not None:
                                ins.then_inc(sems[inc], n)
                    return body
                getattr(blk, en)(mk(en))


def build_nc(phases=("p0", "p1", "p2"), debug=False):
    nc = bass.Bass("TRN2", target_bir_lowering=False)
    resv = nc.alloc_sbuf_tensor("resv_manual", [128, 195840], mybir.dt.int8)
    rm = nc.lookup_mloc(resv)
    assert rm.addr <= PER and rm.addr + 195840 >= PER + 195712, rm.addr
    ikind = "ExternalOutput" if debug else "Internal"

    xT_d = nc.dram_tensor("xT", [NSEQ, D, S], F32, kind="ExternalInput").ap()
    wqkvT_d = nc.dram_tensor("wqkvT", [D, 3 * D], F32, kind="ExternalInput").ap()
    woutT_d = nc.dram_tensor("woutT", [D, D], F32, kind="ExternalInput").ap()
    wrT_d = nc.dram_tensor("wrT", [D, NE], F32, kind="ExternalInput").ap()
    vecs_d = nc.dram_tensor("vecs", [128, 24], F32, kind="ExternalInput").ap()
    ident_d = nc.dram_tensor("ident", [128, 128], F32, kind="ExternalInput").ap()
    w1s_d = nc.dram_tensor("w1s", [FF, D], F32, kind="ExternalInput").ap()
    w2s_d = nc.dram_tensor("w2s", [D, FF], F32, kind="ExternalInput").ap()
    we1_d = nc.dram_tensor("we1", [NE, FF, D], F32, kind="ExternalInput").ap()
    we2_d = nc.dram_tensor("we2", [NE, D, FF], F32, kind="ExternalInput").ap()
    outT_d = nc.dram_tensor("outT", [NSEQ, D, S], F32, kind="ExternalOutput").ap()
    bf_i = mybir.dt.uint16 if debug else BF16
    xn_d = nc.dram_tensor("xn_i", [NSEQ, D, S], bf_i, kind=ikind).ap().bitcast(BF16)
    h_d = nc.dram_tensor("h_i", [NSEQ, D, S], bf_i, kind=ikind).ap().bitcast(BF16)
    gate_d = nc.dram_tensor("gate_i", [NE, NSEQ * S], F32, kind=ikind).ap()
    dbg_d = nc.dram_tensor("dbg_i", [10, 512], F32, kind=ikind).ap() if debug else None
    dbg2_d = nc.dram_tensor("dbg2_i", [128, 1024], F32, kind=ikind).ap() if debug else None
    q1_d = nc.dram_tensor("q1_i", [9, FF, D], bf_i, kind=ikind).ap().bitcast(BF16)
    q2_d = nc.dram_tensor("q2_i", [9, D, FF], bf_i, kind=ikind).ap().bitcast(BF16)

    def sb(name, shape, dtype, off):
        assert PER <= off and off < PER + 195712, (name, off)
        return nc.alloc_sbuf_tensor_at(name, shape, dtype, offset=off).ap()

    vecs = sb("vecs_s", [128, 24], F32, OFF_VECS)
    wrp = sb("wrp_s", [128, NE, NE], F32, OFF_WRP)
    wrt = sb("wrt_s", [128, NE, NE], F32, OFF_WRT)
    rbias = sb("rbias_s", [NE, 1], F32, OFF_RBIAS)
    ones32 = sb("ones32", [128, 1], F32, OFF_ONES32)
    ones32c = sb("ones32c", [1, 128], F32, OFF_ONES32C)
    onesr = sb("onesr", [1, 128], F32R, OFF_ONESR)
    ident = sb("ident_s", [128, 128], F32, OFF_IDENT)
    sc1 = sb("sc1_s", [128, 9, 16], F32, OFF_SC1)
    sc2 = sb("sc2_s", [128, 9, 8], F32, OFF_SC2)
    wbuf = sb("wbuf", [128, 2, 2048], F32, OFF_WBUF)
    pb = sb("pb", [128, 2, 2048], BF16, OFF_PB)
    nb = sb("nb", [128, 2048], BF16, OFF_NB)
    qsc = sb("qsc", [128, 8], F32, OFF_QSC)
    gatebc = sb("gatebc", [128, 1024], BF16, OFF_GATEBC)
    grow = sb("grow", [1, 1024], F32, OFF_GROW)
    growr = sb("growr", [1, 1024], F32R, OFF_GROWR)

    xbuf = sb("xbuf", [128, 2, 8, 512], F32, U0 + P0_XBUF)
    sqb = sb("sqb", [128, 2, 512], F32, U0 + P0_SQ)
    ubuf = sb("ubuf", [128, 2, 512], F32, U0 + P0_UB)
    u1b = sb("u1b", [128, 512], F32, U0 + P0_U1)
    bct = sb("bct", [128, 4, 512], F32, U0 + P0_BCT)
    xnst = sb("xnst", [128, 2, 8, 512], BF16, U0 + P0_XNST)
    hst = sb("hst", [128, 2, 8, 512], BF16, U0 + P0_HST)
    rows = sb("rows", [1, 8, 512], F32, U0 + P0_ROWS)
    logitT = sb("logitT", [NE, NSEQ * S], F32, U0 + P0_LOGIT)
    probsu = sb("probsu", [128, 8], F32, U0 + P0_SMALL)
    srow = sb("srow", [128, 4], F32, U0 + P0_SMALL + 256)
    gate_tm = sb("gate_tm", [128, 8], F32, U0 + P0_SMALL + 512)
    gtmp = sb("gtmp", [NE, 128], F32, U0 + P0_SMALL + 1024)
    dbg2s = sb("dbg2s", [128, 8], F32, U0 + P0_SMALL + 1536)

    qT = sb("qT_s", [128, 8, 2048], BF16, U0 + P1_QT)
    kT = sb("kT_s", [128, 8, 2048], BF16, U0 + P1_KT)
    vP = sb("vP_s", [128, 16, 1040], BF16, U0 + P1_VP)
    xo = sb("xo_s", [128, 8, 2048], BF16, U0 + P1_XO)
    eb = sb("eb_s", [128, 2, 1024], BF16, U0 + P1_EB)
    bcn = sb("bcn_s", [64, 2, 512], F32, U0 + P1_BCN)
    rrow = sb("rrow_s", [1, 1024], F32, U0 + P1_RROW)
    rrowr = sb("rrowr_s", [1, 1024], F32R, U0 + P1_RROWR)
    wqf32 = sb("wqf32", [128, 2, 8, 128], F32, U0 + P1A_WQF32)
    wqbf = sb("wqbf", [128, 2, 8, 128], BF16, U0 + P1A_WQBF)
    wvf32 = sb("wvf32", [128, 8, 256], F32, U0 + P1V_WVF32)
    wvbf = sb("wvbf", [128, 8, 1024], BF16, U0 + P1V_WVBF)
    wof32 = sb("wof32", [128, 2, 8, 128], F32, U0 + P1C_WOF32)
    wobf = sb("wobf", [128, 2, 8, 128], BF16, U0 + P1C_WOBF)
    xstr = sb("xstr", [128, 2, 2048], F32, U0 + P1C_XSTR)
    outst = sb("outst", [128, 2, 2048], F32, U0 + P1C_OUTST)

    q1T = sb("q1T_s", [128, 8, 2048], BF16, U0 + P2_Q1)
    q2T = sb("q2T_s", [128, 16, 1024], BF16, U0 + P2_Q2)
    h_s = sb("h_s", [128, 8, 2048], BF16, U0 + P2_H)
    g1g = sb("g1g_s", [128, 16, 1024], BF16, U0 + P2_G1)
    acc = sb("acc_s", [128, 8, 1024], F32, U0 + P2_ACC)

    ps1 = nc.alloc_psum_tensor("ps1", [128, 2048], F32).ap()
    ps2 = nc.alloc_psum_tensor("ps2", [128, 2048], F32).ap()

    kb = KB(nc)
    C = kb.c

    def dview(ap3, seq):
        return ap3[seq].rearrange("(c p) t -> p c t", p=128)

    t1 = kb.dma('sync', vecs, vecs_d)
    t2 = kb.dma('sync', wrt, wrT_d.rearrange("(c p) e -> p c e", p=128))
    t3 = kb.dma('sync', ident, ident_d)
    sq_setup = [t1, t2, t3]
    kb.op('gpsimd', lambda e: e.memset(ones32, 1.0), inc='pool')
    kb.op('gpsimd', lambda e: e.memset(ones32c, 1.0), inc='pool')
    pool_ones = C('pool')
    kb.op('vector', lambda e: e.tensor_copy(onesr, ones32c),
          waits=[('pool', pool_ones)], inc='dve')
    for cc in range(NCC):
        kb.op('vector', lambda e, o=wrp[:, cc, :], i=wrt[:, cc, :], g=vecs[:, cc:cc + 1]:
              e.tensor_single_scalar(o, i, g, ALU.mult),
              waits=sq_setup, inc='dve')
    dve_wrp = C('dve')
    if "p0lite" not in phases:
        for cc in range(NCC):
            kb.op('tensor', lambda e, i=wrt[:, cc, :], r=vecs[:, 8 + cc:8 + cc + 1], cc=cc:
                  e.matmul(ps1[0:NE, 1536:1537], i, r, start=(cc == 0), stop=(cc == NCC - 1)),
                  waits=sq_setup, inc='pe')
        pe_rbias = C('pe')
        kb.op('scalar', lambda e: e.copy(rbias, ps1[0:NE, 1536:1537]),
              waits=[('pe', pe_rbias)], inc='act')

    qchunks = []
    for m in range(9):
        w1src = w1s_d if m == 0 else we1_d[m - 1]
        w2src = w2s_d if m == 0 else we2_d[m - 1]
        for rc in range(NFC):
            qchunks.append(('q1', m, rc, w1src[rc * 128:(rc + 1) * 128, :], D))
        for rc in range(NDC):
            qchunks.append(('q2', m, rc, w2src[rc * 128:(rc + 1) * 128, :], FF))
    qst = {'i': 0, 'consumed': {}, 'store': {}}
    qstore_done = {}

    def pump_quant(k):
        if "p2" not in phases:
            return
        for _ in range(k):
            i = qst['i']
            if i >= len(qchunks):
                return
            kind, m, rc, src, Cw = qchunks[i]
            p = i % 2
            w_ap = wbuf[:, p, 0:Cw]
            waits = [('dve', qst['consumed'].get(i - 2, 0))]
            ld = kb.dma('sync', w_ap, src, waits=waits)
            red = qsc[:, 4 * p:4 * p + 1]
            thr = qsc[:, 4 * p + 1:4 * p + 2]
            nthr = qsc[:, 4 * p + 2:4 * p + 3]
            kb.op('vector', lambda e, o=red, s=w_ap: e.reduce_sum(
                o, s, axis=AX.X, apply_absolute_value=True), waits=[ld], inc='dve')
            scdst = sc1[:, m, rc:rc + 1] if kind == 'q1' else sc2[:, m, rc:rc + 1]
            kb.op('vector', lambda e, o=scdst, s=red, c=1.0 / Cw: e.tensor_scalar(
                o, s, c, 1e-5, ALU.mult, ALU.max), waits=[('dve', C('dve'))], inc='dve')
            kb.op('vector', lambda e, o=thr, s=scdst: e.tensor_scalar_mul(o, s, 0.5),
                  waits=[('dve', C('dve'))], inc='dve')
            kb.op('vector', lambda e, o=nthr, s=scdst: e.tensor_scalar_mul(o, s, -0.5),
                  inc='dve')
            dve_thr = C('dve')
            pbx = pb[:, p, 0:Cw]
            kb.op('vector', lambda e, o=pbx, s=w_ap, t=thr: e.tensor_single_scalar(
                o, s, t, ALU.is_gt),
                waits=[('dve', dve_thr), qst['store'].get(i - 2, ('pe', 0))], inc='dve')
            kb.op('vector', lambda e, o=nb[:, 0:Cw], s=w_ap, t=nthr: e.tensor_single_scalar(
                o, s, t, ALU.is_lt), inc='dve')
            kb.op('vector', lambda e, o=pbx, a=pbx, b=nb[:, 0:Cw]: e.tensor_tensor(
                o, a, b, ALU.subtract), inc='dve')
            qst['consumed'][i] = C('dve')
            dst = (q1_d if kind == 'q1' else q2_d)[m][rc * 128:(rc + 1) * 128, :]
            st = kb.dma('sync', dst, pbx, waits=[('dve', qst['consumed'][i])])
            qst['store'][i] = st
            qstore_done.setdefault((kind, m), []).append(st)
            qst['i'] = i + 1

    x_consumed = {}
    st_stage = {}
    xn_toks = {}
    h_toks = {}
    rows_read = {}
    bct_done = {}
    sq_x = {}
    pe_sq2 = {}
    act_sq = {}
    pe_router = {}
    pe_router_c = {}
    act_logit = {}

    def xsrc(tc):
        seq, ts = tc // 4, (tc % 4) * 512
        return dview(xT_d, seq)[:, :, ts:ts + 512]

    sq_x[0] = kb.dma('sync', xbuf[:, 0], xsrc(0))
    for tc in range(8):
        p = tc % 2
        if tc + 1 < 8:
            sq_x[tc + 1] = kb.dma('sync', xbuf[:, (tc + 1) % 2], xsrc(tc + 1),
                                  waits=[('dve', x_consumed.get(tc - 1, 0))])
        for cc in range(NCC):
            sp = cc % 2
            w = [sq_x[tc]]
            if (tc, cc - 2) in pe_sq2:
                w.append(('pe', pe_sq2[(tc, cc - 2)]))
            elif tc > 0:
                w.append(('pe', pe_router[tc - 1]))
            kb.op('scalar', lambda e, o=sqb[:, sp, :], i=xbuf[:, p, cc, :]:
                  e.activation(o, i, AF.Square), waits=w, inc='act')
            act_sq[(tc, cc)] = C('act')
            w1 = [sq_x[tc]]
            if cc == 0:
                w1.append(('dve', rows_read.get(tc - 1, 0)))
                if tc == 0:
                    w1.append(('pool', pool_ones))
            kb.op('tensor', lambda e, r=xbuf[:, p, cc, :], cc=cc: e.matmul(
                ps1[0:1, 0:512], ones32, r, start=(cc == 0), stop=(cc == NCC - 1)),
                waits=w1, inc='pe')
            kb.op('tensor', lambda e, r=sqb[:, sp, :], cc=cc: e.matmul(
                ps1[0:1, 512:1024], ones32, r, start=(cc == 0), stop=(cc == NCC - 1)),
                waits=[('act', act_sq[(tc, cc)])], inc='pe')
            pe_sq2[(tc, cc)] = C('pe')

        pe_stat = C('pe')
        r = lambda k: rows[:, k, :]
        kb.op('vector', lambda e: e.tensor_scalar_mul(r(0), ps1[0:1, 0:512], 1.0 / D),
              waits=[('pe', pe_stat)], inc='dve')
        kb.op('vector', lambda e: e.tensor_scalar_mul(r(1), ps1[0:1, 512:1024], 1.0 / D),
              inc='dve')
        rows_read[tc] = C('dve')
        kb.op('vector', lambda e: e.tensor_mul(r(2), r(0), r(0)),
              waits=[('dve', C('dve'))], inc='dve')
        kb.op('vector', lambda e: e.tensor_sub(r(1), r(1), r(2)),
              waits=[('dve', C('dve'))], inc='dve')
        kb.op('vector', lambda e: e.tensor_scalar_add(r(2), r(1), LN_EPS),
              waits=[('dve', C('dve'))], inc='dve')
        dve_a = C('dve')
        kb.op('scalar', lambda e: e.activation(r(3), r(2), AF.Sqrt),
              waits=[('dve', dve_a)], inc='act')
        act_s1 = C('act')
        kb.op('vector', lambda e: e.reciprocal(r(4), r(3)), waits=[('act', act_s1)], inc='dve')
        kb.op('vector', lambda e: e.reciprocal(r(5), r(2)), inc='dve')
        kb.op('vector', lambda e: e.tensor_mul(r(5), r(5), r(1)),
              waits=[('dve', C('dve'))], inc='dve')
        kb.op('vector', lambda e: e.tensor_scalar_add(r(5), r(5), LN_EPS),
              waits=[('dve', C('dve'))], inc='dve')
        dve_b = C('dve')
        kb.op('scalar', lambda e: e.activation(r(6), r(5), AF.Sqrt),
              waits=[('dve', dve_b)], inc='act')
        act_s2 = C('act')
        kb.op('vector', lambda e: e.reciprocal(r(7), r(6)), waits=[('act', act_s2)], inc='dve')
        kb.op('vector', lambda e: e.tensor_mul(r(7), r(7), r(4)),
              waits=[('dve', C('dve'))], inc='dve')
        kb.op('vector', lambda e: e.tensor_mul(r(5), r(0), r(4)),
              waits=[('dve', C('dve'))], inc='dve')
        kb.op('vector', lambda e: e.tensor_mul(r(6), r(0), r(7)),
              waits=[('dve', C('dve'))], inc='dve')
        dve_rows = C('dve')
        for k, ridx in enumerate([4, 5, 7, 6]):
            w = [('dve', dve_rows)]
            if tc > 0:
                w.append(('act', bct_done[tc - 1]))
            else:
                w.append(('pool', pool_ones))
            kb.op('tensor', lambda e, o=ps2[:, k * 512:(k + 1) * 512], rr=rows[:, ridx, :]:
                  e.matmul(o, ones32c, rr, start=True, stop=True), waits=w, inc='pe')
        pe_bc = C('pe')
        for k in range(4):
            kb.op('scalar', lambda e, o=bct[:, k, :], i=ps2[:, k * 512:(k + 1) * 512]:
                  e.copy(o, i), waits=[('pe', pe_bc)], inc='act')
        bct_done[tc] = C('act')
        for cc in range(NCC):
            up = cc % 2
            w = [('act', bct_done[tc])]
            if cc == 0:
                w.extend(st_stage.get(tc - 2, []))
            kb.op('vector', lambda e, i=xbuf[:, p, cc, :]: e.tensor_mul(u1b, i, bct[:, 0, :]),
                  waits=w, inc='dve')
            kb.op('vector', lambda e, o=xnst[:, p, cc, :]: e.tensor_sub(o, u1b, bct[:, 1, :]),
                  inc='dve')
            w2 = []
            if cc >= 2:
                w2.append(('pe', pe_router_c[(tc, cc - 2)]))
            kb.op('vector', lambda e, o=ubuf[:, up, :], i=xbuf[:, p, cc, :]:
                  e.tensor_mul(o, i, bct[:, 2, :]), waits=w2, inc='dve')
            kb.op('vector', lambda e, o=ubuf[:, up, :]: e.tensor_sub(o, o, bct[:, 3, :]),
                  inc='dve')
            kb.op('vector', lambda e, o=hst[:, p, cc, :], i=ubuf[:, up, :],
                  g=vecs[:, cc:cc + 1], b=vecs[:, 8 + cc:8 + cc + 1]:
                  e.tensor_scalar(o, i, g, b, ALU.mult, ALU.add), inc='dve')
            dve_u = C('dve')
            w3 = [('dve', dve_u), ('dve', dve_wrp)]
            if cc == 0 and tc > 0:
                w3.append(('act', act_logit[tc - 1]))
            kb.op('tensor', lambda e, i=wrp[:, cc, :], rr=ubuf[:, up, :], cc=cc:
                  e.matmul(ps1[0:NE, 1024:1536], i, rr, start=(cc == 0), stop=(cc == NCC - 1)),
                  waits=w3, inc='pe')
            pe_router_c[(tc, cc)] = C('pe')
        pe_router[tc] = C('pe')
        x_consumed[tc] = C('dve')
        dve_norm = C('dve')
        kb.op('scalar', lambda e, o=logitT[:, tc * 512:(tc + 1) * 512]:
              e.copy(o, ps1[0:NE, 1024:1536]), waits=[('pe', pe_router[tc])], inc='act')
        act_logit[tc] = C('act')
        seq, ts = tc // 4, (tc % 4) * 512
        t_xn = kb.dma('sync', dview(xn_d, seq)[:, :, ts:ts + 512], xnst[:, p],
                      waits=[('dve', dve_norm)])
        t_h = kb.dma('sync', dview(h_d, seq)[:, :, ts:ts + 512], hst[:, p],
                     waits=[('dve', dve_norm)])
        st_stage[tc] = [t_xn, t_h]
        xn_toks.setdefault(seq, []).append(t_xn)
        h_toks.setdefault(seq, []).append(t_h)
        pump_quant(3)

    kb.op('vector', lambda e: e.tensor_single_scalar(logitT, logitT, rbias, ALU.add),
          waits=[('act', C('act'))], inc='dve')
    dve_lbias = C('dve')
    pe_T2 = {}
    act_exp0 = {}
    act_gtmp = {}
    dve_g = {}
    gate_store = 0
    gstore_sq = {}
    gate_toks = []
    for tb in range(32):
        w = [('dve', dve_lbias), ('act', act_exp0.get(tb - 1, 0))]
        kb.op('tensor', lambda e, i=logitT[:, tb * 128:(tb + 1) * 128]:
              e.matmul(ps1[0:128, 0:8], i, ident[0:NE, 0:NE], start=True, stop=True),
              waits=w, inc='pe')
        pe_t1 = C('pe')
        w = [('pe', pe_t1), ('dve', dve_g.get(tb - 1, 0))]
        kb.op('scalar', lambda e: e.activation(probsu, ps1[0:128, 0:8], AF.Exp),
              waits=w, inc='act')
        act_exp0[tb] = C('act')
        kb.op('vector', lambda e: e.reduce_sum(srow[:, 0:1], probsu, axis=AX.X),
              waits=[('act', act_exp0[tb])], inc='dve')
        kb.op('vector', lambda e: e.reciprocal(srow[:, 1:2], srow[:, 0:1]),
              waits=[('dve', C('dve'))], inc='dve')
        kb.op('vector', lambda e: e.reduce_max(srow[:, 2:3], probsu, axis=AX.X),
              waits=[('dve', C('dve'))], inc='dve')
        kb.op('vector', lambda e: e.tensor_mul(srow[:, 3:4], srow[:, 2:3], srow[:, 1:2]),
              waits=[('dve', C('dve'))], inc='dve')
        kb.op('vector', lambda e: e.tensor_scalar(gate_tm, probsu, srow[:, 2:3], srow[:, 3:4],
                                                  ALU.is_equal, ALU.mult),
              waits=[('pe', pe_T2.get(tb - 1, 0)), ('dve', C('dve'))], inc='dve')
        dve_g[tb] = C('dve')
        w = [('dve', dve_g[tb]), ('act', act_gtmp.get(tb - 1, 0))]
        kb.op('tensor', lambda e: e.matmul(ps1[0:NE, 512:640], gate_tm, ident),
              waits=w, inc='pe')
        pe_T2[tb] = C('pe')
        w = [('pe', pe_T2[tb]), gstore_sq.get(tb - 1, ('pe', 0))]
        kb.op('scalar', lambda e: e.copy(gtmp, ps1[0:NE, 512:640]), waits=w, inc='act')
        act_gtmp[tb] = C('act')
        gate_store = kb.dma('sync', gate_d[:, tb * 128:(tb + 1) * 128], gtmp,
                            waits=[('act', act_gtmp[tb])])
        gstore_sq[tb] = gate_store
        gate_toks.append(gate_store)
    fin_pe = pe_T2[31]
    fin_act = act_gtmp[31]

    B_done = {}
    C_store_toks = {0: [], 1: []}
    C_mm_done = {}
    C_stt_done = {}
    C_store_done = {}
    A_qk_evac = {}
    A_v_evac = {}
    norm_dve_last = 0

    for seq in (range(NSEQ) if "p1" in phases else []):
        w = kb.maxtok(xn_toks[seq])
        if seq == 0:
            w += [('pe', fin_pe), ('act', fin_act)] + kb.maxtok(gate_toks)
        else:
            w.append(('pe', C_mm_done[seq - 1]))
        sq_xn = kb.dma('sync', xo, dview(xn_d, seq), waits=w)
        pool_cast = {}
        pe_grp = {}
        dve_evac = {}
        for j in range(16):
            jp = j % 2
            fbase = j * 128
            w = [('pool', pool_cast.get(j - 2, 0))]
            if j < 2 and seq > 0:
                w.append(('pe', C_mm_done[seq - 1]))
            ldw = kb.dma('sync', wqf32[:, jp],
                         wqkvT_d[:, fbase:fbase + 128].rearrange("(c p) f -> p c f", p=128),
                         waits=w)
            wc = [ldw, ('pe', pe_grp.get(j - 2, 0))]
            if j < 2 and seq > 0:
                wc.append(('pe', C_mm_done[seq - 1]))
            kb.op('gpsimd', lambda e, o=wqbf[:, jp], i=wqf32[:, jp]: e.tensor_copy(o, i),
                  waits=wc, inc='pool')
            pool_cast[j] = C('pool')
            psq = ps1 if jp == 0 else ps2
            for cc in range(NCC):
                for tq in range(4):
                    w = []
                    if cc == 0 and tq == 0:
                        w = [('pool', pool_cast[j]), sq_xn,
                             ('dve', dve_evac.get(j - 2, 0))]
                        if j < 2:
                            if seq == 0:
                                w.append(('act', fin_act))
                            else:
                                w.append(('dve', C_stt_done[seq - 1]))
                    last = (cc == NCC - 1 and tq == 3)
                    kb.op('tensor', lambda e, o=psq[:, tq * 512:(tq + 1) * 512],
                          i=wqbf[:, jp, cc, :], rr=xo[:, cc, tq * 512:(tq + 1) * 512], cc=cc:
                          e.matmul(o, i, rr, start=(cc == 0), stop=(cc == NCC - 1)),
                          waits=w, inc='pe' if last else None)
            pe_grp[j] = C('pe')
            dst = qT[:, j, :] if j < 8 else kT[:, j - 8, :]
            w = [('pe', pe_grp[j])]
            if seq > 0 and j >= 8 and j < 10:
                w.extend(kb.maxtok(C_store_toks[seq - 1]))
            kb.op('vector', lambda e, o=dst, i=psq: e.tensor_copy(o, i), waits=w, inc='dve')
            dve_evac[j] = C('dve')
            pump_quant(1)
        A_qk_evac[seq] = C('dve')
        pool_vc = {}
        for rnd in range(4):
            w = [('pool', pool_vc.get(rnd - 1, 0))]
            if rnd == 0:
                w.append(('pool', pool_cast[15]))
            if rnd == 0 and seq == 0:
                w += [('pe', fin_pe), ('act', fin_act)]
            if rnd == 0 and seq > 0:
                w.append(('pe', C_mm_done[seq - 1]))
            ldv = kb.dma('sync', wvf32,
                         wqkvT_d[:, 2048 + rnd * 256:2048 + (rnd + 1) * 256]
                         .rearrange("(c p) f -> p c f", p=128), waits=w)
            wc = [ldv]
            if rnd == 0:
                wc.append(('pe', pe_grp[15]))
            if rnd == 0 and seq > 0:
                wc.append(('pe', C_mm_done[seq - 1]))
            kb.op('gpsimd', lambda e, o=wvbf[:, :, rnd * 256:(rnd + 1) * 256], i=wvf32:
                  e.tensor_copy(o, i), waits=wc, inc='pool')
            pool_vc[rnd] = C('pool')
        pool_vcast = C('pool')
        w = kb.maxtok(xn_toks[1] + h_toks[1]) + [('pe', fin_pe)] if seq == 0 else [('pe', B_done[seq - 1])]
        kb.op('gpsimd', lambda e: e.memset(
            vP.rearrange("p k (h x) -> p k h x", h=16)[:, :, :, 64:65], 1.0),
            waits=w, inc='pool')
        pool_vones = C('pool')
        act_vevac = {}
        for tb in range(16):
            tp = tb % 2
            psv = ps1[:, tp * 1024:(tp + 1) * 1024]
            for cc in range(NCC):
                for g in range(2):
                    w = []
                    if cc == 0 and g == 0:
                        w = [('pool', pool_vcast), ('act', act_vevac.get(tb - 2, 0)),
                             ('dve', A_qk_evac[seq])]
                    last = (cc == NCC - 1 and g == 1)
                    kb.op('tensor', lambda e, o=psv[:, g * 512:(g + 1) * 512],
                          i=xo[:, cc, tb * 128:(tb + 1) * 128],
                          rr=wvbf[:, cc, g * 512:(g + 1) * 512], cc=cc:
                          e.matmul(o, i, rr, start=(cc == 0), stop=(cc == NCC - 1)),
                          waits=w, inc='pe' if last else None)
            pe_v = C('pe')
            kb.op('scalar', lambda e,
                  o=vP[:, tb, :].rearrange("p (h x) -> p h x", h=16)[:, :, 0:64],
                  i=psv.rearrange("p (h d) -> p h d", h=16): e.copy(o, i),
                  waits=[('pe', pe_v), ('pool', pool_vones)], inc='act')
            act_vevac[tb] = C('act')
        A_v_evac[seq] = C('act')
        pump_quant(2)

        skipB = "p1nob" in phases
        exp_done = {}
        pvb_done = {}
        prev_exp_tail = 0
        bcn_copy_prev = 0
        for hp in (range(HP) if not skipB else []):
            hA, hB = 2 * hp, 2 * hp + 1
            for qb in range(NQB):
                qs = qb * QB
                for kbk in range(NKB):
                    par = kbk % 2
                    ks = kbk * 128
                    spair = ps1[:, par * 1024:(par + 1) * 1024]
                    if kbk < 2:
                        w = [('act', prev_exp_tail)]
                        if kbk == 0:
                            w += [('dve', A_qk_evac[seq]), ('dve', norm_dve_last),
                                  ('act', A_v_evac[seq])]
                    else:
                        w = [('act', exp_done[(hp, qb, kbk - 2)])]
                    kb.op('tensor', lambda e, o=spair[:, 0:512], i=kT[0:64, hp, ks:ks + 128],
                          rr=qT[0:64, hp, qs:qs + 512]:
                          e.matmul(o, i, rr, start=True, stop=True), waits=w, inc=None)
                    kb.op('tensor', lambda e, o=spair[:, 512:1024],
                          i=kT[64:128, hp, ks:ks + 128], rr=qT[64:128, hp, qs:qs + 512]:
                          e.matmul(o, i, rr, start=True, stop=True, tile_position=(64, 0)),
                          inc='pe')
                    pe_sb = C('pe')
                    kb.op('scalar', lambda e, o=eb[:, par, :], i=spair: e.activation(
                        o, i, AF.Exp, scale=0.125), waits=[('pe', pe_sb)], inc='act')
                    exp_done[(hp, qb, kbk)] = C('act')
                    w = [('act', exp_done[(hp, qb, kbk)])]
                    if kbk == 0:
                        w.append(('act', A_v_evac[seq]))
                    kb.op('tensor', lambda e, i=vP[:, kbk, hA * 65:hA * 65 + 65],
                          rr=eb[:, par, 0:512], kbk=kbk: e.matmul(
                              ps2[0:65, 0:512], i, rr, start=(kbk == 0), stop=(kbk == NKB - 1)),
                          waits=w, inc=None)
                    kb.op('tensor', lambda e, i=vP[:, kbk, hB * 65:hB * 65 + 65],
                          rr=eb[:, par, 512:1024], kbk=kbk: e.matmul(
                              ps2[0:65, 512:1024], i, rr, start=(kbk == 0), stop=(kbk == NKB - 1)),
                          inc='pe')
                    pvb_done[(hp, qb, kbk)] = C('pe')
                prev_exp_tail = exp_done[(hp, qb, NKB - 1)]
                pe_pv = pvb_done[(hp, qb, NKB - 1)]
                kb.op('vector', lambda e: e.reciprocal(rrow[:, 0:512], ps2[64:65, 0:512]),
                      waits=[('pe', pe_pv)], inc='dve')
                kb.op('vector', lambda e: e.reciprocal(rrow[:, 512:1024], ps2[64:65, 512:1024]),
                      inc='dve')
                kb.op('vector', lambda e: e.tensor_copy(rrowr, rrow),
                      waits=[('dve', C('dve'))], inc='dve')
                dve_rr = C('dve')
                kb.op('tensor', lambda e: e.matmul(ps2[0:64, 1024:1536], onesr[:, 0:64],
                                                   rrowr[:, 0:512], start=True, stop=True),
                      waits=[('dve', dve_rr), ('act', bcn_copy_prev)], inc=None)
                kb.op('tensor', lambda e: e.matmul(ps2[0:64, 1536:2048], onesr[:, 0:64],
                                                   rrowr[:, 512:1024], start=True, stop=True),
                      inc='pe')
                pe_bcn = C('pe')
                kb.op('scalar', lambda e: e.copy(bcn.rearrange("p a x -> p (a x)"),
                                                 ps2[0:64, 1024:2048]),
                      waits=[('pe', pe_bcn)], inc='act')
                bcn_copy_prev = C('act')
                kb.op('vector', lambda e, o=xo[0:64, hp, qs:qs + 512]: e.tensor_mul(
                    o, ps2[0:64, 0:512], bcn[:, 0, :]),
                    waits=[('act', bcn_copy_prev)], inc='dve')
                kb.op('vector', lambda e, o=xo[64:128, hp, qs:qs + 512]: e.tensor_mul(
                    o, ps2[0:64, 512:1024], bcn[:, 1, :]), inc='dve')
                norm_dve_last = C('dve')
                pump_quant(2)
        B_done[seq] = C('pe')

        skipC = "p1noc" in phases
        pool_oc = {}
        pe_ogrp = {}
        dve_stt = {}
        st_out = ('pe', 0)
        st_toks = {}
        for fc in (range(NCC) if not skipC else []):
            fp = fc % 2
            w = [('pool', pool_oc.get(fc - 2, 0))]
            if fc < 2:
                w += [('pe', B_done[seq]), ('dve', norm_dve_last)]
            ldo = kb.dma('sync', wof32[:, fp],
                         woutT_d[:, fc * 128:(fc + 1) * 128].rearrange("(c p) f -> p c f", p=128),
                         waits=w)
            wc = [ldo, ('pe', pe_ogrp.get(fc - 2, 0))]
            if fc < 2:
                wc.append(('pe', B_done[seq]))
            kb.op('gpsimd', lambda e, o=wobf[:, fp], i=wof32[:, fp]: e.tensor_copy(o, i),
                  waits=wc, inc='pool')
            pool_oc[fc] = C('pool')
            w = [('dve', dve_stt.get(fc - 2, 0)), ('pe', B_done[seq])]
            if seq > 0 and fc < 2:
                w.extend(kb.maxtok(C_store_toks[seq - 1]))
            sq_xt = kb.dma('sync', xstr[:, fp, :], xT_d[seq, fc * 128:(fc + 1) * 128, :],
                           waits=w)
            psq = ps1 if fp == 0 else ps2
            for cc in range(NCC):
                for tq in range(4):
                    w = []
                    if cc == 0 and tq == 0:
                        w = [('pool', pool_oc[fc]), ('dve', norm_dve_last),
                             ('dve', dve_stt.get(fc - 2, 0))]
                    last = (cc == NCC - 1 and tq == 3)
                    kb.op('tensor', lambda e, o=psq[:, tq * 512:(tq + 1) * 512],
                          i=wobf[:, fp, cc, :], rr=xo[:, cc, tq * 512:(tq + 1) * 512], cc=cc:
                          e.matmul(o, i, rr, start=(cc == 0), stop=(cc == NCC - 1)),
                          waits=w, inc='pe' if last else None)
            pe_ogrp[fc] = C('pe')
            kb.op('vector', lambda e, o=outst[:, fp, :], i=psq, b=vecs[:, 16 + fc:16 + fc + 1],
                  xx=xstr[:, fp, :]: e.scalar_tensor_tensor(o, i, b, xx, ALU.add, ALU.add),
                  waits=[('pe', pe_ogrp[fc]), sq_xt, st_toks.get(fc - 2, ('pe', 0))],
                  inc='dve')
            dve_stt[fc] = C('dve')
            st_out = kb.dma('sync', outT_d[seq, fc * 128:(fc + 1) * 128, :], outst[:, fp, :],
                            waits=[('dve', dve_stt[fc])])
            st_toks[fc] = st_out
            pump_quant(1)
        C_mm_done[seq] = C('pe')
        C_stt_done[seq] = C('dve')
        C_store_done[seq] = st_out if not skipC else ('pe', 0)
        C_store_toks[seq] = list(st_toks.values())

    pump_quant(len(qchunks))
    pool_qdone = C('pool')
    dve_qdone = C('dve')

    def xbar_q1(m, waits):
        toks = []
        for cc in range(NCC):
            toks.append(kb.dma('sync', q1T[:, cc, :], q1_d[m][:, cc * 128:(cc + 1) * 128],
                               waits=waits if cc == 0 else (), transpose=True))
        return kb.maxtok(toks)

    def xbar_q2(m, waits):
        toks = []
        for fc in range(NFC):
            toks.append(kb.dma('sync', q2T[:, fc, :], q2_d[m][:, fc * 128:(fc + 1) * 128],
                               waits=waits if fc == 0 else (), transpose=True))
        return kb.maxtok(toks)

    if "p2" in phases:
        cq_q1 = xbar_q1(0, kb.maxtok(qstore_done[('q1', 0)]) + [('dve', C_stt_done[1]),
                            ('pe', B_done[1])])
        cq_q2 = xbar_q2(0, kb.maxtok(qstore_done[('q2', 0)] + C_store_toks[1]) +
                        [('pe', B_done[1])])

    iters = [(s, t) for s in range(NSEQ) for t in range(2)] if "p2" in phases else []
    prev_acc_store = ('pe', 0)
    sq_h = 0
    h_users_pe = 0
    dve_acc_last = 0
    act_silu = {}
    dve_gate = {}
    pe_w1_last = {}
    pe_w2_last = {}
    dve_stt2 = {}
    dve_gr_prev = 0
    act_gbc = 0
    prev_silu_tail = 0
    prev_stt2_tail = 0
    final_st = 0
    for it, (seq, tt) in enumerate(iters):
        if tt == 0:
            sq_h = kb.dma('sync', h_s, dview(h_d, seq),
                          waits=kb.maxtok(h_toks[seq]) + [('pe', B_done[1]),
                                                          ('pe', h_users_pe)])
        sq_acc = kb.dma('sync', acc, dview(outT_d, seq)[:, :, tt * TT:(tt + 1) * TT],
                        waits=kb.maxtok(C_store_toks[seq]) + [('dve', dve_acc_last),
                               ('pe', C_mm_done[1]), prev_acc_store])
        for m in range(9):
            if m > 0:
                w = kb.maxtok(gate_toks) + [('dve', dve_gr_prev)]
                if it == 0 and m == 1:
                    w.append(('dve', dve_qdone))
                sq_g = kb.dma('sync', grow,
                              gate_d[m - 1:m, seq * S + tt * TT:seq * S + (tt + 1) * TT],
                              waits=w)
                kb.op('vector', lambda e: e.tensor_copy(growr, grow),
                      waits=[sq_g], inc='dve')
                dve_gr_prev = C('dve')
                kb.op('tensor', lambda e: e.matmul(ps1[:, 0:512], onesr, growr[:, 0:512],
                                                   start=True, stop=True),
                      waits=[('dve', dve_gr_prev), ('act', prev_silu_tail)], inc=None)
                kb.op('tensor', lambda e: e.matmul(ps1[:, 512:1024], onesr, growr[:, 512:1024],
                                                   start=True, stop=True), inc='pe')
                pe_gbc = C('pe')
                w = [('pe', pe_gbc), ('dve', dve_gate.get((it, m - 1, NFC - 1), 0))]
                if it == 0 and m == 1:
                    w.append(('dve', dve_qdone))
                kb.op('scalar', lambda e: e.copy(gatebc, ps1[:, 0:1024]), waits=w, inc='act')
                act_gbc = C('act')
            for fc in range(NFC):
                fp = fc % 2
                psq = ps1[:, fp * 1024:(fp + 1) * 1024]
                for cc in range(NCC):
                    for tq in range(2):
                        w = []
                        if cc == 0 and tq == 0:
                            w = list(cq_q1) + [sq_h]
                            if fc >= 2:
                                w.append(('act', act_silu[(it, m, fc - 2)]))
                            else:
                                w.append(('act', max(prev_silu_tail, act_gbc)))
                        last = (cc == NCC - 1 and tq == 1)
                        kb.op('tensor', lambda e, o=psq[:, tq * 512:(tq + 1) * 512],
                              i=q1T[:, cc, fc * 128:(fc + 1) * 128],
                              rr=h_s[:, cc, tt * TT + tq * 512:tt * TT + (tq + 1) * 512], cc=cc:
                              e.matmul(o, i, rr, start=(cc == 0), stop=(cc == NCC - 1)),
                              waits=w, inc='pe' if last else None)
                pe_g = C('pe')
                w = [('pe', pe_g)]
                if m == 0:
                    w.append(('pe', pe_w2_last.get((it - 1, 8), 0)))
                else:
                    w.append(('pe', pe_w2_last.get((it, m - 1), 0)))
                kb.op('scalar', lambda e, o=g1g[:, fc, :], i=psq, s=sc1[:, m, fc:fc + 1]:
                      e.activation(o, i, AF.Silu, scale=s), waits=w, inc='act')
                act_silu[(it, m, fc)] = C('act')
                if m > 0:
                    kb.op('vector', lambda e, o=g1g[:, fc, :]: e.tensor_mul(o, o, gatebc),
                          waits=[('act', act_silu[(it, m, fc)])], inc='dve')
                dve_gate[(it, m, fc)] = C('dve')
            pe_w1_last[(it, m)] = C('pe')
            prev_silu_tail = act_silu[(it, m, NFC - 1)]
            if not (it == len(iters) - 1 and m == 8):
                nm = m + 1 if m + 1 <= 8 else 0
                cq_q1 = xbar_q1(nm, kb.maxtok(qstore_done[('q1', nm)]) +
                                [('pe', pe_w1_last[(it, m)])])
            for dc in range(NDC):
                dp = dc % 2
                psq2 = ps2[:, dp * 1024:(dp + 1) * 1024]
                for fc in range(NFC):
                    for tq in range(2):
                        w = []
                        if fc == 0 and tq == 0:
                            w = list(cq_q2)
                            if dc >= 2:
                                w.append(('dve', dve_stt2[(it, m, dc - 2)]))
                            else:
                                w.append(('dve', prev_stt2_tail))
                        if tq == 0:
                            if m > 0:
                                w.append(('dve', dve_gate[(it, m, fc)]))
                            else:
                                w.append(('act', act_silu[(it, m, fc)]))
                        last = (fc == NFC - 1 and tq == 1)
                        kb.op('tensor', lambda e, o=psq2[:, tq * 512:(tq + 1) * 512],
                              i=q2T[:, fc, dc * 128:(dc + 1) * 128],
                              rr=g1g[:, fc, tq * 512:(tq + 1) * 512], fc=fc:
                              e.matmul(o, i, rr, start=(fc == 0), stop=(fc == NFC - 1)),
                              waits=w, inc='pe' if last else None)
                pe_g2 = C('pe')
                kb.op('vector', lambda e, o=acc[:, dc, :], i=psq2, s=sc2[:, m, dc:dc + 1],
                      a=acc[:, dc, :]: e.scalar_tensor_tensor(o, i, s, a, ALU.mult, ALU.add),
                      waits=[('pe', pe_g2), sq_acc], inc='dve')
                dve_stt2[(it, m, dc)] = C('dve')
            pe_w2_last[(it, m)] = C('pe')
            prev_stt2_tail = dve_stt2[(it, m, NDC - 1)]
            if not (it == len(iters) - 1 and m == 8):
                nm = m + 1 if m + 1 <= 8 else 0
                cq_q2 = xbar_q2(nm, kb.maxtok(qstore_done[('q2', nm)]) +
                                [('pe', pe_w2_last[(it, m)])])
        dve_acc_last = C('dve')
        h_users_pe = C('pe')
        final_st = kb.dma('sync', dview(outT_d, seq)[:, :, tt * TT:(tt + 1) * TT], acc,
                          waits=[('dve', dve_acc_last)])
        prev_acc_store = final_st

    kb.op('sync', lambda e: e.nop(), waits=kb.all_dma() + [('act', kb.c('act')),
                                            ('pe', kb.c('pe')), ('dve', kb.c('dve')),
                                            ('pool', kb.c('pool'))])

    import contextlib
    with contextlib.ExitStack() as ctx:
        sems = {}
        names = ['pe', 'act', 'dve', 'pool']
        names += ['sq%d' % i for i in range(KB.NQSEM)]
        names += ['cq%d' % i for i in range(KB.NQSEM)]
        for name in names:
            sems[name] = ctx.enter_context(nc.semaphore(name))
        kb.build(sems)
    return nc


_NC_CACHE = None


def kernel(**inputs):
    global _NC_CACHE
    x = np.ascontiguousarray(np.asarray(inputs["L0_attn_x"], dtype=np.float32))
    w_qkv = np.asarray(inputs["L0_attn_w_qkv"], dtype=np.float32)
    w_out = np.asarray(inputs["L0_attn_w_out"], dtype=np.float32)
    b_out = np.asarray(inputs["L0_attn_b_out"], dtype=np.float32)
    w1s = np.ascontiguousarray(np.asarray(inputs["L0_ffn_shared_w1"], dtype=np.float32))
    w2s = np.ascontiguousarray(np.asarray(inputs["L0_ffn_shared_w2"], dtype=np.float32))
    we1 = np.ascontiguousarray(np.asarray(inputs["L0_ffn_expert_w1"], dtype=np.float32))
    we2 = np.ascontiguousarray(np.asarray(inputs["L0_ffn_expert_w2"], dtype=np.float32))
    wr = np.asarray(inputs["L0_ffn_router_w"], dtype=np.float32)
    gamma = np.asarray(inputs["L0_ffn_ln_gamma"], dtype=np.float32)
    beta = np.asarray(inputs["L0_ffn_ln_beta"], dtype=np.float32)

    wqkvT = np.ascontiguousarray(w_qkv.T)
    woutT = np.ascontiguousarray(w_out.T)
    wrT = np.ascontiguousarray(wr.T)
    vecs = np.zeros((128, 24), np.float32)
    vecs[:, 0:8] = gamma.reshape(8, 128).T
    vecs[:, 8:16] = beta.reshape(8, 128).T
    vecs[:, 16:24] = b_out.reshape(8, 128).T
    ident = np.eye(128, dtype=np.float32)

    if _NC_CACHE is None:
        _NC_CACHE = build_nc()
    nc = _NC_CACHE

    in_maps = []
    for k in range(NCORES):
        xk = np.ascontiguousarray(x[2 * k:2 * k + 2].transpose(0, 2, 1))
        in_maps.append({
            "xT": xk, "wqkvT": wqkvT, "woutT": woutT, "wrT": wrT,
            "vecs": vecs, "ident": ident,
            "w1s": w1s, "w2s": w2s, "we1": we1, "we2": we2,
        })
    res = run_bass_kernel_spmd(nc, in_maps, core_ids=list(range(NCORES)))
    outs = []
    for k in range(NCORES):
        o = res.results[k]["outT"]
        outs.append(np.asarray(o).transpose(0, 2, 1))
    return np.ascontiguousarray(np.concatenate(outs, axis=0))


# version 05
# speedup vs baseline: 2.0109x; 1.3163x over previous
import numpy as np
import ml_dtypes
import concourse.bass as bass
import concourse.mybir as mybir
from concourse.bass_utils import run_bass_kernel_spmd

F32 = mybir.dt.float32
F32R = mybir.dt.float32r
BF16 = mybir.dt.bfloat16
AF = mybir.ActivationFunctionType
ALU = mybir.AluOpType
AX = mybir.AxisListType

NCORES = 8
NSEQ = 2
S = 2048
D = 1024
NCC = 8
H = 16
HP = 8
HD = 64
QB = 512
NQB = S // QB
NKB = S // 128
NE = 8
FF = 2048
NFC = FF // 128
NDC = D // 128
TT = 1024
LN_EPS = 1e-5

TOP = 212352
PER = 16640
OFF_VECS = PER
OFF_WRP = PER + 128
OFF_WRT = OFF_WRP + 256
OFF_RBIAS = OFF_WRT + 256
OFF_ONES32 = OFF_RBIAS + 32
OFF_ONES32C = OFF_ONES32 + 32
OFF_ONESR = OFF_ONES32C + 512
OFF_IDENT = OFF_ONESR + 512
OFF_SC1 = OFF_IDENT + 512
OFF_SC2 = OFF_SC1 + 576
PERSIST_END = OFF_SC2 + 320
QR = PERSIST_END
OFF_WBUF = QR
OFF_PB = QR + 16384
OFF_NB = OFF_PB + 8192
OFF_QSC = OFF_NB + 4096
QUANT_END = OFF_QSC + 64
OFF_GATEBC = QR
OFF_GROW = QR + 2048
OFF_GROWR = QR + 6144
U0 = ((QUANT_END + 63) // 64) * 64
assert U0 + 163840 <= TOP, (U0, TOP)

P0_XBUF = 0
P0_SQ = 32768
P0_UB = 36864
P0_U1 = 40960
P0_BCT = 43008
P0_XNST = 51200
P0_HST = 67584
P0_ROWS = 83968
P0_LOGIT = 100352
P0_SMALL = 116736

P1_QT = 0
P1_KT = 32768
P1_VP = 65536
P1_XO = 98816
P1_EB = 131584
P1_BCN = 135680
P1_RROW = 139776
P1_RROWR = 143872
P1A_WQF32 = 131584
P1A_WQBF = 139776
P1V_WVF32 = 131584
P1V_WVBF = 139776
P1C_WOF32 = 131584
P1C_WOBF = 139776
P1C_XSTR = 0
P1C_OUTST = 32768

P2_Q1 = 0
P2_Q2 = 32768
P2_H = 65536
P2_G1 = 98304
P2_ACC = 131072


class KB:
    ENGINES = ("sync", "scalar", "vector", "tensor", "gpsimd")

    def __init__(self, nc):
        self.nc = nc
        self.ops = []
        self.cnt = {}

    ENGSEM = {'vector': 'dve', 'scalar': 'act', 'gpsimd': 'pool'}

    def op(self, engine, fn, waits=(), inc=None, n=1):
        ws = []
        for s, v in waits:
            if v <= 0:
                continue
            assert v <= self.cnt.get(s, 0), (engine, s, v, self.cnt.get(s, 0))
            ws.append((s, v))
        es = self.ENGSEM.get(engine)
        if es is not None:
            prev = self.cnt.get(es, 0)
            if prev > 0 and not any(s == es and v >= prev for (s, v) in ws):
                ws.append((es, prev))
            if inc is None:
                inc, n = es, 1
        self.ops.append((engine, fn, tuple(ws), inc, n))
        if inc is not None:
            self.cnt[inc] = self.cnt.get(inc, 0) + n
            return self.cnt[inc]
        return None

    NQSEM = 6

    def dma(self, q, out, in_, waits=(), sem=None, transpose=False):
        if not hasattr(self, 'qrr'):
            self.qrr = {}
        i = self.qrr.get(q, 0)
        self.qrr[q] = (i + 1) % self.NQSEM
        snm = ('sq%d' if q == 'sync' else 'cq%d') % i
        w = list(waits)
        prev = self.cnt.get(snm, 0)
        if prev > 0:
            w.append((snm, prev))
        if transpose:
            fn = lambda e, o=out, i=in_: e.dma_start_transpose(o, i)
        else:
            fn = lambda e, o=out, i=in_: e.dma_start(out=o, in_=i)
        v = self.op(q, fn, w, inc=snm, n=16)
        return (snm, v)

    @staticmethod
    def maxtok(toks):
        best = {}
        for (nm, v) in toks:
            if v > best.get(nm, 0):
                best[nm] = v
        return list(best.items())

    def all_dma(self):
        toks = []
        for i in range(self.NQSEM):
            for pre in ('sq', 'cq'):
                nm = '%s%d' % (pre, i)
                if self.cnt.get(nm, 0) > 0:
                    toks.append((nm, self.cnt[nm]))
        return toks

    def c(self, sem):
        return self.cnt.get(sem, 0)

    def build(self, sems):
        used = {e for (e, *_r) in self.ops}
        with self.nc.Block() as blk:
            for en in self.ENGINES:
                if en not in used:
                    continue

                def mk(en):
                    def body(eng):
                        for (e, fn, waits, inc, n) in self.ops:
                            if e != en:
                                continue
                            for (s, v) in waits:
                                eng.wait_ge(sems[s], v)
                            ins = fn(eng)
                            if inc is not None:
                                ins.then_inc(sems[inc], n)
                    return body
                getattr(blk, en)(mk(en))


def build_nc(phases=("p0", "p1", "p2"), debug=False):
    nc = bass.Bass("TRN2", target_bir_lowering=False)
    resv = nc.alloc_sbuf_tensor("resv_manual", [128, 195840], mybir.dt.int8)
    rm = nc.lookup_mloc(resv)
    assert rm.addr <= PER and rm.addr + 195840 >= PER + 195712, rm.addr
    ikind = "ExternalOutput" if debug else "Internal"

    xT_d = nc.dram_tensor("xT", [NSEQ, D, S], F32, kind="ExternalInput").ap()
    wqkvT_d = nc.dram_tensor("wqkvT", [D, 3 * D], F32, kind="ExternalInput").ap()
    woutT_d = nc.dram_tensor("woutT", [D, D], F32, kind="ExternalInput").ap()
    wrT_d = nc.dram_tensor("wrT", [D, NE], F32, kind="ExternalInput").ap()
    vecs_d = nc.dram_tensor("vecs", [128, 24], F32, kind="ExternalInput").ap()
    ident_d = nc.dram_tensor("ident", [128, 128], F32, kind="ExternalInput").ap()
    w1s_d = nc.dram_tensor("w1s", [FF, D], F32, kind="ExternalInput").ap()
    w2s_d = nc.dram_tensor("w2s", [D, FF], F32, kind="ExternalInput").ap()
    we1_d = nc.dram_tensor("we1", [NE, FF, D], F32, kind="ExternalInput").ap()
    we2_d = nc.dram_tensor("we2", [NE, D, FF], F32, kind="ExternalInput").ap()
    outT_d = nc.dram_tensor("outT", [NSEQ, D, S], F32, kind="ExternalOutput").ap()
    bf_i = mybir.dt.uint16 if debug else BF16
    xn_d = nc.dram_tensor("xn_i", [NSEQ, D, S], bf_i, kind=ikind).ap().bitcast(BF16)
    h_d = nc.dram_tensor("h_i", [NSEQ, D, S], bf_i, kind=ikind).ap().bitcast(BF16)
    gate_d = nc.dram_tensor("gate_i", [NE, NSEQ * S], mybir.dt.uint16,
                            kind=ikind).ap().bitcast(BF16)
    dbg_d = nc.dram_tensor("dbg_i", [10, 512], F32, kind=ikind).ap() if debug else None
    dbg2_d = nc.dram_tensor("dbg2_i", [128, 1024], F32, kind=ikind).ap() if debug else None
    bcn_d = nc.dram_tensor("bcn_i", [2, 1024], F32, kind="Internal").ap()
    q1_d = nc.dram_tensor("q1_i", [9, FF, D], bf_i, kind=ikind).ap().bitcast(BF16)
    q2_d = nc.dram_tensor("q2_i", [9, D, FF], bf_i, kind=ikind).ap().bitcast(BF16)

    def sb(name, shape, dtype, off):
        assert PER <= off and off < PER + 195712, (name, off)
        return nc.alloc_sbuf_tensor_at(name, shape, dtype, offset=off).ap()

    vecs = sb("vecs_s", [128, 24], F32, OFF_VECS)
    wrp = sb("wrp_s", [128, NE, NE], F32, OFF_WRP)
    wrt = sb("wrt_s", [128, NE, NE], F32, OFF_WRT)
    rbias = sb("rbias_s", [NE, 1], F32, OFF_RBIAS)
    ones32 = sb("ones32", [128, 1], F32, OFF_ONES32)
    ones32c = sb("ones32c", [1, 128], F32, OFF_ONES32C)
    onesr = sb("onesr", [1, 128], F32R, OFF_ONESR)
    ident = sb("ident_s", [128, 128], F32, OFF_IDENT)
    sc1 = sb("sc1_s", [128, 9, 16], F32, OFF_SC1)
    sc2 = sb("sc2_s", [128, 9, 8], F32, OFF_SC2)
    wbuf = sb("wbuf", [128, 2, 2048], F32, OFF_WBUF)
    pb = sb("pb", [128, 2, 2048], BF16, OFF_PB)
    nb = sb("nb", [128, 2048], BF16, OFF_NB)
    qsc = sb("qsc", [128, 8], F32, OFF_QSC)
    gatebc = sb("gatebc", [128, 1024], BF16, OFF_GATEBC)
    grow = sb("grow", [1, 1024], F32, OFF_GROW)
    growr = sb("growr", [1, 1024], F32R, OFF_GROWR)

    xbuf = sb("xbuf", [128, 2, 8, 512], F32, U0 + P0_XBUF)
    sqb = sb("sqb", [128, 2, 512], F32, U0 + P0_SQ)
    ubuf = sb("ubuf", [128, 2, 512], F32, U0 + P0_UB)
    u1b = sb("u1b", [128, 512], F32, U0 + P0_U1)
    bct = sb("bct", [128, 4, 512], F32, U0 + P0_BCT)
    xnst = sb("xnst", [128, 2, 8, 512], BF16, U0 + P0_XNST)
    hst = sb("hst", [128, 2, 8, 512], BF16, U0 + P0_HST)
    rows = sb("rows", [1, 8, 512], F32, U0 + P0_ROWS)
    logitT = sb("logitT", [NE, NSEQ * S], F32, U0 + P0_LOGIT)
    probsu = sb("probsu", [128, 8], F32, U0 + P0_SMALL)
    srow = sb("srow", [128, 4], F32, U0 + P0_SMALL + 256)
    gate_tm = sb("gate_tm", [128, 8], F32, U0 + P0_SMALL + 512)
    gtmp = sb("gtmp", [NE, 128], BF16, U0 + P0_SMALL + 1024)
    dbg2s = sb("dbg2s", [128, 8], F32, U0 + P0_SMALL + 1536)

    qT = sb("qT_s", [128, 8, 2048], BF16, U0 + P1_QT)
    kT = sb("kT_s", [128, 8, 2048], BF16, U0 + P1_KT)
    vP = sb("vP_s", [128, 16, 1040], BF16, U0 + P1_VP)
    xo = sb("xo_s", [128, 8, 2048], BF16, U0 + P1_XO)
    eb = sb("eb_s", [128, 2, 1024], BF16, U0 + P1_EB)
    bcn = sb("bcn_s", [64, 2, 1024], F32, U0 + P1_BCN)
    rrow = sb("rrow_s", [1, 2, 1024], F32, U0 + P1_BCN + 8192)
    wqf32 = sb("wqf32", [128, 2, 8, 128], F32, U0 + P1A_WQF32)
    wqbf = sb("wqbf", [128, 2, 8, 128], BF16, U0 + P1A_WQBF)
    wvf32 = sb("wvf32", [128, 8, 256], F32, U0 + P1V_WVF32)
    wvbf = sb("wvbf", [128, 8, 1024], BF16, U0 + P1V_WVBF)
    wof32 = sb("wof32", [128, 2, 8, 128], F32, U0 + P1C_WOF32)
    wobf = sb("wobf", [128, 2, 8, 128], BF16, U0 + P1C_WOBF)
    xstr = sb("xstr", [128, 2, 2048], F32, U0 + P1C_XSTR)
    outst = sb("outst", [128, 2, 2048], F32, U0 + P1C_OUTST)

    q1T = sb("q1T_s", [128, 8, 2048], BF16, U0 + P2_Q1)
    q2T = sb("q2T_s", [128, 16, 1024], BF16, U0 + P2_Q2)
    h_s = sb("h_s", [128, 8, 2048], BF16, U0 + P2_H)
    g1g = sb("g1g_s", [128, 16, 1024], BF16, U0 + P2_G1)
    acc = sb("acc_s", [128, 8, 1024], F32, U0 + P2_ACC)

    ps1 = nc.alloc_psum_tensor("ps1", [128, 2048], F32).ap()
    ps2 = nc.alloc_psum_tensor("ps2", [128, 2048], F32).ap()

    kb = KB(nc)
    C = kb.c

    def dview(ap3, seq):
        return ap3[seq].rearrange("(c p) t -> p c t", p=128)

    t1 = kb.dma('sync', vecs, vecs_d)
    t2 = kb.dma('sync', wrt, wrT_d.rearrange("(c p) e -> p c e", p=128))
    t3 = kb.dma('sync', ident, ident_d)
    sq_setup = [t1, t2, t3]
    kb.op('gpsimd', lambda e: e.memset(ones32, 1.0), inc='pool')
    kb.op('gpsimd', lambda e: e.memset(ones32c, 1.0), inc='pool')
    pool_ones = C('pool')
    kb.op('vector', lambda e: e.tensor_copy(onesr, ones32c),
          waits=[('pool', pool_ones)], inc='dve')
    for cc in range(NCC):
        kb.op('vector', lambda e, o=wrp[:, cc, :], i=wrt[:, cc, :], g=vecs[:, cc:cc + 1]:
              e.tensor_single_scalar(o, i, g, ALU.mult),
              waits=sq_setup, inc='dve')
    dve_wrp = C('dve')
    if "p0lite" not in phases:
        for cc in range(NCC):
            kb.op('tensor', lambda e, i=wrt[:, cc, :], r=vecs[:, 8 + cc:8 + cc + 1], cc=cc:
                  e.matmul(ps1[0:NE, 1536:1537], i, r, start=(cc == 0), stop=(cc == NCC - 1)),
                  waits=sq_setup, inc='pe')
        pe_rbias = C('pe')
        kb.op('scalar', lambda e: e.copy(rbias, ps1[0:NE, 1536:1537]),
              waits=[('pe', pe_rbias)], inc='act')

    qchunks = []
    for m in range(9):
        w1src = w1s_d if m == 0 else we1_d[m - 1]
        w2src = w2s_d if m == 0 else we2_d[m - 1]
        for rc in range(NFC):
            qchunks.append(('q1', m, rc, w1src[rc * 128:(rc + 1) * 128, :], D))
        for rc in range(NDC):
            qchunks.append(('q2', m, rc, w2src[rc * 128:(rc + 1) * 128, :], FF))
    qst = {'i': 0, 'consumed': {}, 'store': {}}
    qstore_done = {}

    def pump_quant(k):
        if "p2" not in phases:
            return
        for _ in range(k):
            i = qst['i']
            if i >= len(qchunks):
                return
            kind, m, rc, src, Cw = qchunks[i]
            p = i % 2
            w_ap = wbuf[:, p, 0:Cw]
            waits = [('dve', qst['consumed'].get(i - 2, 0))]
            ld = kb.dma('sync', w_ap, src, waits=waits)
            red = qsc[:, 4 * p:4 * p + 1]
            thr = qsc[:, 4 * p + 1:4 * p + 2]
            nthr = qsc[:, 4 * p + 2:4 * p + 3]
            kb.op('vector', lambda e, o=red, s=w_ap: e.reduce_sum(
                o, s, axis=AX.X, apply_absolute_value=True), waits=[ld], inc='dve')
            scdst = sc1[:, m, rc:rc + 1] if kind == 'q1' else sc2[:, m, rc:rc + 1]
            kb.op('vector', lambda e, o=scdst, s=red, c=1.0 / Cw: e.tensor_scalar(
                o, s, c, 1e-5, ALU.mult, ALU.max), waits=[('dve', C('dve'))], inc='dve')
            kb.op('vector', lambda e, o=thr, s=scdst: e.tensor_scalar_mul(o, s, 0.5),
                  waits=[('dve', C('dve'))], inc='dve')
            kb.op('vector', lambda e, o=nthr, s=scdst: e.tensor_scalar_mul(o, s, -0.5),
                  inc='dve')
            dve_thr = C('dve')
            pbx = pb[:, p, 0:Cw]
            kb.op('vector', lambda e, o=pbx, s=w_ap, t=thr: e.tensor_single_scalar(
                o, s, t, ALU.is_gt),
                waits=[('dve', dve_thr), qst['store'].get(i - 2, ('pe', 0))], inc='dve')
            kb.op('vector', lambda e, o=nb[:, 0:Cw], s=w_ap, t=nthr: e.tensor_single_scalar(
                o, s, t, ALU.is_lt), inc='dve')
            kb.op('vector', lambda e, o=pbx, a=pbx, b=nb[:, 0:Cw]: e.tensor_tensor(
                o, a, b, ALU.subtract), inc='dve')
            qst['consumed'][i] = C('dve')
            dst = (q1_d if kind == 'q1' else q2_d)[m][rc * 128:(rc + 1) * 128, :]
            st = kb.dma('sync', dst, pbx, waits=[('dve', qst['consumed'][i])])
            qst['store'][i] = st
            qstore_done.setdefault((kind, m), []).append(st)
            qst['i'] = i + 1

    x_consumed = {}
    st_stage = {}
    xn_toks = {}
    h_toks = {}
    rows_read = {}
    bct_done = {}
    sq_x = {}
    pe_sq2 = {}
    act_sq = {}
    pe_router = {}
    pe_router_c = {}
    act_logit = {}

    def xsrc(tc):
        seq, ts = tc // 4, (tc % 4) * 512
        return dview(xT_d, seq)[:, :, ts:ts + 512]

    sq_x[0] = kb.dma('sync', xbuf[:, 0], xsrc(0))
    for tc in range(8):
        p = tc % 2
        if tc + 1 < 8:
            sq_x[tc + 1] = kb.dma('sync', xbuf[:, (tc + 1) % 2], xsrc(tc + 1),
                                  waits=[('dve', x_consumed.get(tc - 1, 0))])
        for cc in range(NCC):
            sp = cc % 2
            w = [sq_x[tc]]
            if (tc, cc - 2) in pe_sq2:
                w.append(('pe', pe_sq2[(tc, cc - 2)]))
            elif tc > 0:
                w.append(('pe', pe_router[tc - 1]))
            kb.op('scalar', lambda e, o=sqb[:, sp, :], i=xbuf[:, p, cc, :]:
                  e.activation(o, i, AF.Square), waits=w, inc='act')
            act_sq[(tc, cc)] = C('act')
            w1 = [sq_x[tc]]
            if cc == 0:
                w1.append(('dve', rows_read.get(tc - 1, 0)))
                if tc == 0:
                    w1.append(('pool', pool_ones))
            kb.op('tensor', lambda e, r=xbuf[:, p, cc, :], cc=cc: e.matmul(
                ps1[0:1, 0:512], ones32, r, start=(cc == 0), stop=(cc == NCC - 1)),
                waits=w1, inc='pe')
            kb.op('tensor', lambda e, r=sqb[:, sp, :], cc=cc: e.matmul(
                ps1[0:1, 512:1024], ones32, r, start=(cc == 0), stop=(cc == NCC - 1)),
                waits=[('act', act_sq[(tc, cc)])], inc='pe')
            pe_sq2[(tc, cc)] = C('pe')

        pe_stat = C('pe')
        r = lambda k: rows[:, k, :]
        kb.op('vector', lambda e: e.tensor_scalar_mul(r(0), ps1[0:1, 0:512], 1.0 / D),
              waits=[('pe', pe_stat)], inc='dve')
        kb.op('vector', lambda e: e.tensor_scalar_mul(r(1), ps1[0:1, 512:1024], 1.0 / D),
              inc='dve')
        rows_read[tc] = C('dve')
        kb.op('vector', lambda e: e.tensor_mul(r(2), r(0), r(0)),
              waits=[('dve', C('dve'))], inc='dve')
        kb.op('vector', lambda e: e.tensor_sub(r(1), r(1), r(2)),
              waits=[('dve', C('dve'))], inc='dve')
        kb.op('vector', lambda e: e.tensor_scalar_add(r(2), r(1), LN_EPS),
              waits=[('dve', C('dve'))], inc='dve')
        dve_a = C('dve')
        kb.op('scalar', lambda e: e.activation(r(3), r(2), AF.Sqrt),
              waits=[('dve', dve_a)], inc='act')
        act_s1 = C('act')
        kb.op('vector', lambda e: e.reciprocal(r(4), r(3)), waits=[('act', act_s1)], inc='dve')
        kb.op('vector', lambda e: e.reciprocal(r(5), r(2)), inc='dve')
        kb.op('vector', lambda e: e.tensor_mul(r(5), r(5), r(1)),
              waits=[('dve', C('dve'))], inc='dve')
        kb.op('vector', lambda e: e.tensor_scalar_add(r(5), r(5), LN_EPS),
              waits=[('dve', C('dve'))], inc='dve')
        dve_b = C('dve')
        kb.op('scalar', lambda e: e.activation(r(6), r(5), AF.Sqrt),
              waits=[('dve', dve_b)], inc='act')
        act_s2 = C('act')
        kb.op('vector', lambda e: e.reciprocal(r(7), r(6)), waits=[('act', act_s2)], inc='dve')
        kb.op('vector', lambda e: e.tensor_mul(r(7), r(7), r(4)),
              waits=[('dve', C('dve'))], inc='dve')
        kb.op('vector', lambda e: e.tensor_mul(r(5), r(0), r(4)),
              waits=[('dve', C('dve'))], inc='dve')
        kb.op('vector', lambda e: e.tensor_mul(r(6), r(0), r(7)),
              waits=[('dve', C('dve'))], inc='dve')
        dve_rows = C('dve')
        for k, ridx in enumerate([4, 5, 7, 6]):
            w = [('dve', dve_rows)]
            if tc > 0:
                w.append(('act', bct_done[tc - 1]))
            else:
                w.append(('pool', pool_ones))
            kb.op('tensor', lambda e, o=ps2[:, k * 512:(k + 1) * 512], rr=rows[:, ridx, :]:
                  e.matmul(o, ones32c, rr, start=True, stop=True), waits=w, inc='pe')
        pe_bc = C('pe')
        for k in range(4):
            kb.op('scalar', lambda e, o=bct[:, k, :], i=ps2[:, k * 512:(k + 1) * 512]:
                  e.copy(o, i), waits=[('pe', pe_bc)], inc='act')
        bct_done[tc] = C('act')
        for cc in range(NCC):
            up = cc % 2
            w = [('act', bct_done[tc])]
            if cc == 0:
                w.extend(st_stage.get(tc - 2, []))
            kb.op('vector', lambda e, i=xbuf[:, p, cc, :]: e.tensor_mul(u1b, i, bct[:, 0, :]),
                  waits=w, inc='dve')
            kb.op('vector', lambda e, o=xnst[:, p, cc, :]: e.tensor_sub(o, u1b, bct[:, 1, :]),
                  inc='dve')
            w2 = []
            if cc >= 2:
                w2.append(('pe', pe_router_c[(tc, cc - 2)]))
            kb.op('vector', lambda e, o=ubuf[:, up, :], i=xbuf[:, p, cc, :]:
                  e.tensor_mul(o, i, bct[:, 2, :]), waits=w2, inc='dve')
            kb.op('vector', lambda e, o=ubuf[:, up, :]: e.tensor_sub(o, o, bct[:, 3, :]),
                  inc='dve')
            kb.op('vector', lambda e, o=hst[:, p, cc, :], i=ubuf[:, up, :],
                  g=vecs[:, cc:cc + 1], b=vecs[:, 8 + cc:8 + cc + 1]:
                  e.tensor_scalar(o, i, g, b, ALU.mult, ALU.add), inc='dve')
            dve_u = C('dve')
            w3 = [('dve', dve_u), ('dve', dve_wrp)]
            if cc == 0 and tc > 0:
                w3.append(('act', act_logit[tc - 1]))
            kb.op('tensor', lambda e, i=wrp[:, cc, :], rr=ubuf[:, up, :], cc=cc:
                  e.matmul(ps1[0:NE, 1024:1536], i, rr, start=(cc == 0), stop=(cc == NCC - 1)),
                  waits=w3, inc='pe')
            pe_router_c[(tc, cc)] = C('pe')
        pe_router[tc] = C('pe')
        x_consumed[tc] = C('dve')
        dve_norm = C('dve')
        kb.op('scalar', lambda e, o=logitT[:, tc * 512:(tc + 1) * 512]:
              e.copy(o, ps1[0:NE, 1024:1536]), waits=[('pe', pe_router[tc])], inc='act')
        act_logit[tc] = C('act')
        seq, ts = tc // 4, (tc % 4) * 512
        t_xn = kb.dma('sync', dview(xn_d, seq)[:, :, ts:ts + 512], xnst[:, p],
                      waits=[('dve', dve_norm)])
        t_h = kb.dma('sync', dview(h_d, seq)[:, :, ts:ts + 512], hst[:, p],
                     waits=[('dve', dve_norm)])
        st_stage[tc] = [t_xn, t_h]
        xn_toks.setdefault(seq, []).append(t_xn)
        h_toks.setdefault(seq, []).append(t_h)
        pump_quant(3)

    kb.op('vector', lambda e: e.tensor_single_scalar(logitT, logitT, rbias, ALU.add),
          waits=[('act', C('act'))], inc='dve')
    dve_lbias = C('dve')
    pe_T2 = {}
    act_exp0 = {}
    act_gtmp = {}
    dve_g = {}
    gate_store = 0
    gstore_sq = {}
    gate_toks = []
    for tb in range(32):
        w = [('dve', dve_lbias), ('act', act_exp0.get(tb - 1, 0))]
        kb.op('tensor', lambda e, i=logitT[:, tb * 128:(tb + 1) * 128]:
              e.matmul(ps1[0:128, 0:8], i, ident[0:NE, 0:NE], start=True, stop=True),
              waits=w, inc='pe')
        pe_t1 = C('pe')
        w = [('pe', pe_t1), ('dve', dve_g.get(tb - 1, 0))]
        kb.op('scalar', lambda e: e.activation(probsu, ps1[0:128, 0:8], AF.Exp),
              waits=w, inc='act')
        act_exp0[tb] = C('act')
        kb.op('vector', lambda e: e.reduce_sum(srow[:, 0:1], probsu, axis=AX.X),
              waits=[('act', act_exp0[tb])], inc='dve')
        kb.op('vector', lambda e: e.reciprocal(srow[:, 1:2], srow[:, 0:1]),
              waits=[('dve', C('dve'))], inc='dve')
        kb.op('vector', lambda e: e.reduce_max(srow[:, 2:3], probsu, axis=AX.X),
              waits=[('dve', C('dve'))], inc='dve')
        kb.op('vector', lambda e: e.tensor_mul(srow[:, 3:4], srow[:, 2:3], srow[:, 1:2]),
              waits=[('dve', C('dve'))], inc='dve')
        kb.op('vector', lambda e: e.tensor_scalar(gate_tm, probsu, srow[:, 2:3], srow[:, 3:4],
                                                  ALU.is_equal, ALU.mult),
              waits=[('pe', pe_T2.get(tb - 1, 0)), ('dve', C('dve'))], inc='dve')
        dve_g[tb] = C('dve')
        w = [('dve', dve_g[tb]), ('act', act_gtmp.get(tb - 1, 0))]
        kb.op('tensor', lambda e: e.matmul(ps1[0:NE, 512:640], gate_tm, ident),
              waits=w, inc='pe')
        pe_T2[tb] = C('pe')
        w = [('pe', pe_T2[tb]), gstore_sq.get(tb - 1, ('pe', 0))]
        kb.op('scalar', lambda e: e.copy(gtmp, ps1[0:NE, 512:640]), waits=w, inc='act')
        act_gtmp[tb] = C('act')
        gate_store = kb.dma('sync', gate_d[:, tb * 128:(tb + 1) * 128], gtmp,
                            waits=[('act', act_gtmp[tb])])
        gstore_sq[tb] = gate_store
        gate_toks.append(gate_store)
    fin_pe = pe_T2[31]
    fin_act = act_gtmp[31]

    B_done = {}
    C_store_toks = {0: [], 1: []}
    C_mm_done = {}
    C_stt_done = {}
    C_store_done = {}
    A_qk_evac = {}
    A_v_evac = {}
    norm_dve_last = 0

    for seq in (range(NSEQ) if "p1" in phases else []):
        w = kb.maxtok(xn_toks[seq])
        if seq == 0:
            w += [('pe', fin_pe), ('act', fin_act)] + kb.maxtok(gate_toks)
        else:
            w.append(('pe', C_mm_done[seq - 1]))
        sq_xn = kb.dma('sync', xo, dview(xn_d, seq), waits=w)
        pool_cast = {}
        pe_grp = {}
        dve_evac = {}
        for j in range(16):
            jp = j % 2
            fbase = j * 128
            w = [('pool', pool_cast.get(j - 2, 0))]
            if j < 2 and seq > 0:
                w.append(('pe', C_mm_done[seq - 1]))
            ldw = kb.dma('sync', wqf32[:, jp],
                         wqkvT_d[:, fbase:fbase + 128].rearrange("(c p) f -> p c f", p=128),
                         waits=w)
            wc = [ldw, ('pe', pe_grp.get(j - 2, 0))]
            if j < 2 and seq > 0:
                wc.append(('pe', C_mm_done[seq - 1]))
            kb.op('gpsimd', lambda e, o=wqbf[:, jp], i=wqf32[:, jp]: e.tensor_copy(o, i),
                  waits=wc, inc='pool')
            pool_cast[j] = C('pool')
            psq = ps1 if jp == 0 else ps2
            for cc in range(NCC):
                for tq in range(4):
                    w = []
                    if cc == 0 and tq == 0:
                        w = [('pool', pool_cast[j]), sq_xn,
                             ('dve', dve_evac.get(j - 2, 0))]
                        if j < 2:
                            if seq == 0:
                                w.append(('act', fin_act))
                            else:
                                w.append(('dve', C_stt_done[seq - 1]))
                    last = (cc == NCC - 1 and tq == 3)
                    kb.op('tensor', lambda e, o=psq[:, tq * 512:(tq + 1) * 512],
                          i=wqbf[:, jp, cc, :], rr=xo[:, cc, tq * 512:(tq + 1) * 512], cc=cc:
                          e.matmul(o, i, rr, start=(cc == 0), stop=(cc == NCC - 1)),
                          waits=w, inc='pe' if last else None)
            pe_grp[j] = C('pe')
            dst = qT[:, j, :] if j < 8 else kT[:, j - 8, :]
            w = [('pe', pe_grp[j])]
            if seq > 0 and j >= 8 and j < 10:
                w.extend(kb.maxtok(C_store_toks[seq - 1]))
            kb.op('vector', lambda e, o=dst, i=psq: e.tensor_copy(o, i), waits=w, inc='dve')
            dve_evac[j] = C('dve')
            pump_quant(1)
        A_qk_evac[seq] = C('dve')
        pool_vc = {}
        for rnd in range(4):
            w = [('pool', pool_vc.get(rnd - 1, 0))]
            if rnd == 0:
                w.append(('pool', pool_cast[15]))
            if rnd == 0 and seq == 0:
                w += [('pe', fin_pe), ('act', fin_act)]
            if rnd == 0 and seq > 0:
                w.append(('pe', C_mm_done[seq - 1]))
            ldv = kb.dma('sync', wvf32,
                         wqkvT_d[:, 2048 + rnd * 256:2048 + (rnd + 1) * 256]
                         .rearrange("(c p) f -> p c f", p=128), waits=w)
            wc = [ldv]
            if rnd == 0:
                wc.append(('pe', pe_grp[15]))
            if rnd == 0 and seq > 0:
                wc.append(('pe', C_mm_done[seq - 1]))
            kb.op('gpsimd', lambda e, o=wvbf[:, :, rnd * 256:(rnd + 1) * 256], i=wvf32:
                  e.tensor_copy(o, i), waits=wc, inc='pool')
            pool_vc[rnd] = C('pool')
        pool_vcast = C('pool')
        w = kb.maxtok(xn_toks[1] + h_toks[1]) + [('pe', fin_pe)] if seq == 0 else [('pe', B_done[seq - 1])]
        kb.op('gpsimd', lambda e: e.memset(
            vP.rearrange("p k (h x) -> p k h x", h=16)[:, :, :, 64:65], 1.0),
            waits=w, inc='pool')
        pool_vones = C('pool')
        act_vevac = {}
        for tb in range(16):
            tp = tb % 2
            psv = ps1[:, tp * 1024:(tp + 1) * 1024]
            for cc in range(NCC):
                for g in range(2):
                    w = []
                    if cc == 0 and g == 0:
                        w = [('pool', pool_vcast), ('act', act_vevac.get(tb - 2, 0)),
                             ('dve', A_qk_evac[seq])]
                    last = (cc == NCC - 1 and g == 1)
                    kb.op('tensor', lambda e, o=psv[:, g * 512:(g + 1) * 512],
                          i=xo[:, cc, tb * 128:(tb + 1) * 128],
                          rr=wvbf[:, cc, g * 512:(g + 1) * 512], cc=cc:
                          e.matmul(o, i, rr, start=(cc == 0), stop=(cc == NCC - 1)),
                          waits=w, inc='pe' if last else None)
            pe_v = C('pe')
            kb.op('scalar', lambda e,
                  o=vP[:, tb, :].rearrange("p (h x) -> p h x", h=16)[:, :, 0:64],
                  i=psv.rearrange("p (h d) -> p h d", h=16): e.copy(o, i),
                  waits=[('pe', pe_v), ('pool', pool_vones)], inc='act')
            act_vevac[tb] = C('act')
        A_v_evac[seq] = C('act')
        pump_quant(2)

        skipB = "p1nob" in phases
        exp_done = {}
        norm_dve = {}
        bcn_st = {}
        pvb_done = {}
        prev_exp_tail = 0
        tilei = 0
        for hp in (range(HP) if not skipB else []):
            hA, hB = 2 * hp, 2 * hp + 1
            for qb in range(NQB):
                qs = qb * QB
                opp = tilei % 2
                ob = opp * 1024
                tilei += 1

                def emit_pv(kbk, opp=opp, ob=ob):
                    w = [('act', exp_done[(hp, qb, kbk)])]
                    if kbk == 0:
                        w += [('act', A_v_evac[seq]), ('dve', norm_dve.get(opp, 0))]
                    kb.op('tensor', lambda e, i=vP[:, kbk, hA * 65:hA * 65 + 65],
                          rr=eb[:, kbk % 2, 0:512], kbk=kbk: e.matmul(
                              ps2[0:65, ob:ob + 512], i, rr,
                              start=(kbk == 0), stop=(kbk == NKB - 1)),
                          waits=w, inc=None)
                    kb.op('tensor', lambda e, i=vP[:, kbk, hB * 65:hB * 65 + 65],
                          rr=eb[:, kbk % 2, 512:1024], kbk=kbk: e.matmul(
                              ps2[0:65, ob + 512:ob + 1024], i, rr,
                              start=(kbk == 0), stop=(kbk == NKB - 1)),
                          inc='pe')
                    pvb_done[(hp, qb, kbk)] = C('pe')

                for kbk in range(NKB):
                    par = kbk % 2
                    ks = kbk * 128
                    spair = ps1[:, par * 1024:(par + 1) * 1024]
                    if kbk < 2:
                        w = [('act', prev_exp_tail)]
                        if kbk == 0:
                            w += [('dve', A_qk_evac[seq]), ('act', A_v_evac[seq])]
                    else:
                        w = [('act', exp_done[(hp, qb, kbk - 2)])]
                    kb.op('tensor', lambda e, o=spair[:, 0:512], i=kT[0:64, hp, ks:ks + 128],
                          rr=qT[0:64, hp, qs:qs + 512]:
                          e.matmul(o, i, rr, start=True, stop=True), waits=w, inc=None)
                    kb.op('tensor', lambda e, o=spair[:, 512:1024],
                          i=kT[64:128, hp, ks:ks + 128], rr=qT[64:128, hp, qs:qs + 512]:
                          e.matmul(o, i, rr, start=True, stop=True, tile_position=(64, 0)),
                          inc='pe')
                    pe_sb = C('pe')
                    we = [('pe', pe_sb), ('pe', pvb_done.get((hp, qb, kbk - 2), 0))]
                    kb.op('scalar', lambda e, o=eb[:, par, :], i=spair: e.activation(
                        o, i, AF.Exp, scale=0.125), waits=we, inc='act')
                    exp_done[(hp, qb, kbk)] = C('act')
                    if kbk >= 1:
                        emit_pv(kbk - 1)
                emit_pv(NKB - 1)
                prev_exp_tail = exp_done[(hp, qb, NKB - 1)]
                pe_pv = pvb_done[(hp, qb, NKB - 1)]
                kb.op('vector', lambda e, o=rrow[:, opp, :], i=ps2[64:65, ob:ob + 1024]:
                      e.reciprocal(o, i),
                      waits=[('pe', pe_pv), bcn_st.get(opp, ('pe', 0))], inc='dve')
                dve_rr = C('dve')
                t1 = kb.dma('sync', bcn_d[opp], rrow[:, opp, :], waits=[('dve', dve_rr)])
                t2 = kb.dma('sync', bcn[:, opp, :],
                            bcn_d[opp][None, :].broadcast_to([64, 1024]), waits=[t1])
                bcn_st[opp] = t2
                kb.op('vector', lambda e, o=xo[0:64, hp, qs:qs + 512],
                      i=ps2[0:64, ob:ob + 512], b=bcn[:, opp, 0:512]: e.tensor_mul(o, i, b),
                      waits=[t2], inc='dve')
                kb.op('vector', lambda e, o=xo[64:128, hp, qs:qs + 512],
                      i=ps2[0:64, ob + 512:ob + 1024], b=bcn[:, opp, 512:1024]:
                      e.tensor_mul(o, i, b), inc='dve')
                norm_dve[opp] = C('dve')
                norm_dve_last = C('dve')
                pump_quant(2)
        B_done[seq] = C('pe')

        skipC = "p1noc" in phases
        pool_oc = {}
        pe_ogrp = {}
        dve_stt = {}
        st_out = ('pe', 0)
        st_toks = {}
        for fc in (range(NCC) if not skipC else []):
            fp = fc % 2
            w = [('pool', pool_oc.get(fc - 2, 0))]
            if fc < 2:
                w += [('pe', B_done[seq]), ('dve', norm_dve_last)]
            ldo = kb.dma('sync', wof32[:, fp],
                         woutT_d[:, fc * 128:(fc + 1) * 128].rearrange("(c p) f -> p c f", p=128),
                         waits=w)
            wc = [ldo, ('pe', pe_ogrp.get(fc - 2, 0))]
            if fc < 2:
                wc.append(('pe', B_done[seq]))
            kb.op('gpsimd', lambda e, o=wobf[:, fp], i=wof32[:, fp]: e.tensor_copy(o, i),
                  waits=wc, inc='pool')
            pool_oc[fc] = C('pool')
            w = [('dve', dve_stt.get(fc - 2, 0)), ('pe', B_done[seq])]
            if seq > 0 and fc < 2:
                w.extend(kb.maxtok(C_store_toks[seq - 1]))
            sq_xt = kb.dma('sync', xstr[:, fp, :], xT_d[seq, fc * 128:(fc + 1) * 128, :],
                           waits=w)
            psq = ps1 if fp == 0 else ps2
            for cc in range(NCC):
                for tq in range(4):
                    w = []
                    if cc == 0 and tq == 0:
                        w = [('pool', pool_oc[fc]), ('dve', norm_dve_last),
                             ('dve', dve_stt.get(fc - 2, 0))]
                    last = (cc == NCC - 1 and tq == 3)
                    kb.op('tensor', lambda e, o=psq[:, tq * 512:(tq + 1) * 512],
                          i=wobf[:, fp, cc, :], rr=xo[:, cc, tq * 512:(tq + 1) * 512], cc=cc:
                          e.matmul(o, i, rr, start=(cc == 0), stop=(cc == NCC - 1)),
                          waits=w, inc='pe' if last else None)
            pe_ogrp[fc] = C('pe')
            kb.op('vector', lambda e, o=outst[:, fp, :], i=psq, b=vecs[:, 16 + fc:16 + fc + 1],
                  xx=xstr[:, fp, :]: e.scalar_tensor_tensor(o, i, b, xx, ALU.add, ALU.add),
                  waits=[('pe', pe_ogrp[fc]), sq_xt, st_toks.get(fc - 2, ('pe', 0))],
                  inc='dve')
            dve_stt[fc] = C('dve')
            st_out = kb.dma('sync', outT_d[seq, fc * 128:(fc + 1) * 128, :], outst[:, fp, :],
                            waits=[('dve', dve_stt[fc])])
            st_toks[fc] = st_out
            pump_quant(1)
        C_mm_done[seq] = C('pe')
        C_stt_done[seq] = C('dve')
        C_store_done[seq] = st_out if not skipC else ('pe', 0)
        C_store_toks[seq] = list(st_toks.values())

    pump_quant(len(qchunks))
    pool_qdone = C('pool')
    dve_qdone = C('dve')

    def xbar_q1(m, waits):
        toks = []
        for cc in range(NCC):
            toks.append(kb.dma('sync', q1T[:, cc, :], q1_d[m][:, cc * 128:(cc + 1) * 128],
                               waits=waits if cc == 0 else (), transpose=True))
        return kb.maxtok(toks)

    def xbar_q2(m, waits):
        toks = []
        for fc in range(NFC):
            toks.append(kb.dma('sync', q2T[:, fc, :], q2_d[m][:, fc * 128:(fc + 1) * 128],
                               waits=waits if fc == 0 else (), transpose=True))
        return kb.maxtok(toks)

    if "p2" in phases:
        cq_q1 = xbar_q1(0, kb.maxtok(qstore_done[('q1', 0)]) + [('dve', C_stt_done[1]),
                            ('pe', B_done[1])])
        cq_q2 = xbar_q2(0, kb.maxtok(qstore_done[('q2', 0)] + C_store_toks[1]) +
                        [('pe', B_done[1])])

    iters = [(s, t) for s in range(NSEQ) for t in range(2)] if "p2" in phases else []
    prev_acc_store = ('pe', 0)
    sq_h = 0
    h_users_pe = 0
    dve_acc_last = 0
    act_silu = {}
    dve_gate = {}
    pe_w1_last = {}
    pe_w2_last = {}
    dve_stt2 = {}
    dve_gr_prev = 0
    act_gbc = 0
    prev_silu_tail = 0
    prev_stt2_tail = 0
    final_st = 0
    for it, (seq, tt) in enumerate(iters):
        if tt == 0:
            sq_h = kb.dma('sync', h_s, dview(h_d, seq),
                          waits=kb.maxtok(h_toks[seq]) + [('pe', B_done[1]),
                                                          ('pe', h_users_pe)])
        sq_acc = kb.dma('sync', acc, dview(outT_d, seq)[:, :, tt * TT:(tt + 1) * TT],
                        waits=kb.maxtok(C_store_toks[seq]) + [('dve', dve_acc_last),
                               ('pe', C_mm_done[1]), prev_acc_store])
        for m in range(9):
            if m > 0:
                w = kb.maxtok(gate_toks) + [('dve', dve_gate.get((it, m - 1, NFC - 1), 0))]
                if it == 0 and m == 1:
                    w.append(('dve', dve_qdone))
                gb_tok = kb.dma(
                    'sync', gatebc,
                    gate_d[m - 1, seq * S + tt * TT:seq * S + (tt + 1) * TT]
                    [None, :].broadcast_to([128, TT]), waits=w)
            for fc in range(NFC):
                fp = fc % 2
                psq = ps1[:, fp * 1024:(fp + 1) * 1024]
                for cc in range(NCC):
                    for tq in range(2):
                        w = []
                        if cc == 0 and tq == 0:
                            w = list(cq_q1) + [sq_h]
                            if fc >= 2:
                                w.append(('act', act_silu[(it, m, fc - 2)]))
                            else:
                                w.append(('act', prev_silu_tail))
                        last = (cc == NCC - 1 and tq == 1)
                        kb.op('tensor', lambda e, o=psq[:, tq * 512:(tq + 1) * 512],
                              i=q1T[:, cc, fc * 128:(fc + 1) * 128],
                              rr=h_s[:, cc, tt * TT + tq * 512:tt * TT + (tq + 1) * 512], cc=cc:
                              e.matmul(o, i, rr, start=(cc == 0), stop=(cc == NCC - 1)),
                              waits=w, inc='pe' if last else None)
                pe_g = C('pe')
                w = [('pe', pe_g)]
                if m == 0:
                    w.append(('pe', pe_w2_last.get((it - 1, 8), 0)))
                else:
                    w.append(('pe', pe_w2_last.get((it, m - 1), 0)))
                kb.op('scalar', lambda e, o=g1g[:, fc, :], i=psq, s=sc1[:, m, fc:fc + 1]:
                      e.activation(o, i, AF.Silu, scale=s), waits=w, inc='act')
                act_silu[(it, m, fc)] = C('act')
                if m > 0:
                    kb.op('vector', lambda e, o=g1g[:, fc, :]: e.tensor_mul(o, o, gatebc),
                          waits=[('act', act_silu[(it, m, fc)]), gb_tok], inc='dve')
                dve_gate[(it, m, fc)] = C('dve')
            pe_w1_last[(it, m)] = C('pe')
            prev_silu_tail = act_silu[(it, m, NFC - 1)]
            if not (it == len(iters) - 1 and m == 8):
                nm = m + 1 if m + 1 <= 8 else 0
                cq_q1 = xbar_q1(nm, kb.maxtok(qstore_done[('q1', nm)]) +
                                [('pe', pe_w1_last[(it, m)])])
            for dc in range(NDC):
                dp = dc % 2
                psq2 = ps2[:, dp * 1024:(dp + 1) * 1024]
                for fc in range(NFC):
                    for tq in range(2):
                        w = []
                        if fc == 0 and tq == 0:
                            w = list(cq_q2)
                            if dc >= 2:
                                w.append(('dve', dve_stt2[(it, m, dc - 2)]))
                            else:
                                w.append(('dve', prev_stt2_tail))
                        if tq == 0:
                            if m > 0:
                                w.append(('dve', dve_gate[(it, m, fc)]))
                            else:
                                w.append(('act', act_silu[(it, m, fc)]))
                        last = (fc == NFC - 1 and tq == 1)
                        kb.op('tensor', lambda e, o=psq2[:, tq * 512:(tq + 1) * 512],
                              i=q2T[:, fc, dc * 128:(dc + 1) * 128],
                              rr=g1g[:, fc, tq * 512:(tq + 1) * 512], fc=fc:
                              e.matmul(o, i, rr, start=(fc == 0), stop=(fc == NFC - 1)),
                              waits=w, inc='pe' if last else None)
                pe_g2 = C('pe')
                kb.op('vector', lambda e, o=acc[:, dc, :], i=psq2, s=sc2[:, m, dc:dc + 1],
                      a=acc[:, dc, :]: e.scalar_tensor_tensor(o, i, s, a, ALU.mult, ALU.add),
                      waits=[('pe', pe_g2), sq_acc], inc='dve')
                dve_stt2[(it, m, dc)] = C('dve')
            pe_w2_last[(it, m)] = C('pe')
            prev_stt2_tail = dve_stt2[(it, m, NDC - 1)]
            if not (it == len(iters) - 1 and m == 8):
                nm = m + 1 if m + 1 <= 8 else 0
                cq_q2 = xbar_q2(nm, kb.maxtok(qstore_done[('q2', nm)]) +
                                [('pe', pe_w2_last[(it, m)])])
        dve_acc_last = C('dve')
        h_users_pe = C('pe')
        final_st = kb.dma('sync', dview(outT_d, seq)[:, :, tt * TT:(tt + 1) * TT], acc,
                          waits=[('dve', dve_acc_last)])
        prev_acc_store = final_st

    kb.op('sync', lambda e: e.nop(), waits=kb.all_dma() + [('act', kb.c('act')),
                                            ('pe', kb.c('pe')), ('dve', kb.c('dve')),
                                            ('pool', kb.c('pool'))])

    import contextlib
    with contextlib.ExitStack() as ctx:
        sems = {}
        names = ['pe', 'act', 'dve', 'pool']
        names += ['sq%d' % i for i in range(KB.NQSEM)]
        names += ['cq%d' % i for i in range(KB.NQSEM)]
        for name in names:
            sems[name] = ctx.enter_context(nc.semaphore(name))
        kb.build(sems)
    return nc


_NC_CACHE = None


def kernel(**inputs):
    global _NC_CACHE
    x = np.ascontiguousarray(np.asarray(inputs["L0_attn_x"], dtype=np.float32))
    w_qkv = np.asarray(inputs["L0_attn_w_qkv"], dtype=np.float32)
    w_out = np.asarray(inputs["L0_attn_w_out"], dtype=np.float32)
    b_out = np.asarray(inputs["L0_attn_b_out"], dtype=np.float32)
    w1s = np.ascontiguousarray(np.asarray(inputs["L0_ffn_shared_w1"], dtype=np.float32))
    w2s = np.ascontiguousarray(np.asarray(inputs["L0_ffn_shared_w2"], dtype=np.float32))
    we1 = np.ascontiguousarray(np.asarray(inputs["L0_ffn_expert_w1"], dtype=np.float32))
    we2 = np.ascontiguousarray(np.asarray(inputs["L0_ffn_expert_w2"], dtype=np.float32))
    wr = np.asarray(inputs["L0_ffn_router_w"], dtype=np.float32)
    gamma = np.asarray(inputs["L0_ffn_ln_gamma"], dtype=np.float32)
    beta = np.asarray(inputs["L0_ffn_ln_beta"], dtype=np.float32)

    wqkvT = np.ascontiguousarray(w_qkv.T)
    woutT = np.ascontiguousarray(w_out.T)
    wrT = np.ascontiguousarray(wr.T)
    vecs = np.zeros((128, 24), np.float32)
    vecs[:, 0:8] = gamma.reshape(8, 128).T
    vecs[:, 8:16] = beta.reshape(8, 128).T
    vecs[:, 16:24] = b_out.reshape(8, 128).T
    ident = np.eye(128, dtype=np.float32)

    if _NC_CACHE is None:
        _NC_CACHE = build_nc()
    nc = _NC_CACHE

    in_maps = []
    for k in range(NCORES):
        xk = np.ascontiguousarray(x[2 * k:2 * k + 2].transpose(0, 2, 1))
        in_maps.append({
            "xT": xk, "wqkvT": wqkvT, "woutT": woutT, "wrT": wrT,
            "vecs": vecs, "ident": ident,
            "w1s": w1s, "w2s": w2s, "we1": we1, "we2": we2,
        })
    res = run_bass_kernel_spmd(nc, in_maps, core_ids=list(range(NCORES)))
    outs = []
    for k in range(NCORES):
        o = res.results[k]["outT"]
        outs.append(np.asarray(o).transpose(0, 2, 1))
    return np.ascontiguousarray(np.concatenate(outs, axis=0))
